# version 10
# baseline (speedup 1.0000x reference)
import contextlib
import numpy as np

import concourse.bass as bass
import concourse.mybir as mybir
import concourse.bass_utils as _bu
from concourse.bass_utils import run_bass_kernel_spmd

_orig_run_command = _bu.run_command


def _run_command_ldwopt(cmd, *a, **kw):
    cmd = [c.replace("--enable-ldw-opt=false", "--enable-ldw-opt=true")
           if isinstance(c, str) else c for c in cmd]
    return _orig_run_command(cmd, *a, **kw)


_bu.run_command = _run_command_ldwopt

F32 = mybir.dt.float32
BF16 = mybir.dt.bfloat16
AF = mybir.ActivationFunctionType
ALU = mybir.AluOpType
AX = mybir.AxisListType

E = 1024
NE = 8
NEXP = 10
EPS = 1e-5


class Sem:
    def __init__(self, stack, nc, name):
        self.h = stack.enter_context(nc.semaphore(name))
        self.n = 0
        self.hist = {}

    def mark(self, key):
        self.hist[key] = self.n

    def at(self, key):
        return self.hist[key]


class Prog:
    ENGS = ("sync", "gpsimd", "tensor", "vector", "scalar")

    def __init__(self):
        self.ops = {e: [] for e in self.ENGS}

    def emit(self, eng, fn):
        self.ops[eng].append(fn)

    def wait(self, eng, sem, val, future=False):
        assert future or val <= sem.n, f"wait {val} > produced {sem.n} ({eng})"
        if val <= 0:
            return
        self.emit(eng, lambda En, s=sem.h, v=val: En.wait_ge(s, v))

    def flush(self, block):
        for eng in self.ENGS:
            lst = self.ops[eng]
            if not lst:
                continue

            def body(En, lst=lst):
                for fn in lst:
                    fn(En)

            getattr(block, eng)(body)
        self.ops = {e: [] for e in self.ENGS}


def build_nc(T=4096, stage="full"):
    NT = T // 128
    NCH = T // 1024
    KT = NT
    NQC = T // 512
    nc = bass.Bass("TRN2", target_bir_lowering=False)

    x_d = nc.dram_tensor("x", [T, E], F32, kind="ExternalInput").ap()
    wq_d = nc.dram_tensor("wq", [E, E], F32, kind="ExternalInput").ap()
    wk_d = nc.dram_tensor("wk", [E, E], F32, kind="ExternalInput").ap()
    wv_d = nc.dram_tensor("wv", [E, E], F32, kind="ExternalInput").ap()
    rt_d = nc.dram_tensor("rt", [E, 8], F32, kind="ExternalInput").ap()
    eg_d = nc.dram_tensor("eg", [8, E, 512], F32, kind="ExternalInput").ap()
    eu_d = nc.dram_tensor("eu", [8, E, 512], F32, kind="ExternalInput").ap()
    ed_d = nc.dram_tensor("ed", [8, 512, E], F32, kind="ExternalInput").ap()
    sg_d = nc.dram_tensor("sg", [E, E], F32, kind="ExternalInput").ap()
    su_d = nc.dram_tensor("su", [E, E], F32, kind="ExternalInput").ap()
    sd_d = nc.dram_tensor("sd", [E, E], F32, kind="ExternalInput").ap()
    idf_d = nc.dram_tensor("idf", [128, 128], F32, kind="ExternalInput").ap()
    out_d = nc.dram_tensor("out", [T, E], F32, kind="ExternalOutput").ap()
    attn_d = nc.dram_tensor("attn_sc", [T, E], F32,
                            kind="Internal" if stage == "full"
                            else "ExternalOutput").ap()
    dbg_d = (nc.dram_tensor("dbg", [128, NE * T], BF16, kind="ExternalOutput").ap()
             if stage in ("ln", "qkv") else None)
    gbf_d = nc.dram_tensor("gbf", [NEXP, 128, 4096], BF16, kind="Internal").ap()
    ubf_d = nc.dram_tensor("ubf", [NEXP, 128, 4096], BF16, kind="Internal").ap()
    dbf_d = nc.dram_tensor("dbf", [NEXP, 128, 4096], BF16, kind="Internal").ap()

    def psrc(e):
        if e < 2:
            h = e * 512
            return (sg_d[:, h:h + 512], su_d[:, h:h + 512], sd_d[h:h + 512, :])
        return (eg_d[e - 2], eu_d[e - 2], ed_d[e - 2])

    stack = contextlib.ExitStack()
    with stack:
        xnt = stack.enter_context(nc.sbuf_tensor("xnt", [128, NE * T], BF16))
        idf = stack.enter_context(nc.sbuf_tensor("idf_sb", [128, 128], F32))
        stat = stack.enter_context(nc.sbuf_tensor("stat", [128, 32], F32))
        st8 = stack.enter_context(nc.sbuf_tensor("st8", [128, 64], F32))
        rt_bf = stack.enter_context(nc.sbuf_tensor("rt_bf", [128, 64], BF16))
        logit = stack.enter_context(nc.sbuf_tensor("logit", [128, NT * 8], F32))
        coef = stack.enter_context(nc.sbuf_tensor("coef", [128, NT * 8], F32))
        wg = stack.enter_context(nc.sbuf_tensor("wg", [128, 2 * 4096], BF16))
        wu = stack.enter_context(nc.sbuf_tensor("wu", [128, 2 * 4096], BF16))
        wd = stack.enter_context(nc.sbuf_tensor("wd", [128, 2 * 4096], BF16))
        ps2 = stack.enter_context(nc.psum_tensor("ps2", [128, 1024], F32))
        ps3 = stack.enter_context(nc.psum_tensor("ps3", [128, 1024], F32))

        sm = {}
        for name in ("sXNT sLNa sLNb sLNc sTR sMISC "
                     "sWP sWC sQKVm sQKVe sVT sVTe sONE "
                     "sS sEXP sPVA sPVB sNDA sNDB sTRA sTRB sDIV sDV2 "
                     "sWCc sRT sRTe sSMv sSMs "
                     "sTEN sSL sMUL sACC sINIT sOUT").split():
            sm[name] = Sem(stack, nc, name)
        for name in ("sX", "sWCf", "sWCo", "sST", "sFW", "sXR"):
            sm[name + "0"] = Sem(stack, nc, name + "0")
            sm[name + "1"] = Sem(stack, nc, name + "1")
        sm["sST2"] = Sem(stack, nc, "sST2")
        sm["sST3"] = Sem(stack, nc, "sST3")
        S = lambda k: sm[k]
        Sp = lambda k, i: sm[k + str(i % 2)]
        SST = lambda g: sm["sST" + str(g % 4)]

        block = stack.enter_context(nc.Block())
        P = Prog()

        def MM(out_ap, lhsT, rhs, start, stop, sem=None):
            def f(En, o=out_ap, a=lhsT, r=rhs, s1=start, s2=stop, s=sem):
                ins = En.matmul(o, a, r, start=s1, stop=s2, skip_group_check=True)
                if s is not None:
                    ins.then_inc(s.h, 1)
            P.emit("tensor", f)
            if sem is not None:
                sem.n += 1

        def TRN(out_ap, in_ap, np_, sem=None):
            def f(En, o=out_ap, i=in_ap, idn=idf[0:np_, 0:np_], s=sem):
                ins = En.transpose(o, i, idn)
                if s is not None:
                    ins.then_inc(s.h, 1)
            P.emit("tensor", f)
            if sem is not None:
                sem.n += 1

        def ffn_wload(idx, e):
            eb = idx % 2
            for eng in ("sync", "scalar"):
                P.wait(eng, Sp("sWCo", 0), Sp("sWCo", 0).at(12 * e + 10))
                P.wait(eng, Sp("sWCo", 1), Sp("sWCo", 1).at(12 * e + 11))
                if idx >= 2:
                    P.wait(eng, S("sTEN"), S("sTEN").at(("edone", idx - 2)))
            for (eng, wdst, wsrc) in (("sync", "wg", "gbf"),
                                      ("scalar", "wu", "ubf"),
                                      ("sync", "wd", "dbf")):
                wdst = {"wg": wg, "wu": wu, "wd": wd}[wdst]
                wsrc = {"gbf": gbf_d, "ubf": ubf_d, "dbf": dbf_d}[wsrc]
                P.emit(eng, lambda En, e=e, eb=eb, o=wdst, i=wsrc,
                       sh=Sp("sFW", idx).h: En.dma_start(
                    out=o[:, eb * 4096:(eb + 1) * 4096], in_=i[e])
                    .then_inc(sh, 16))
                Sp("sFW", idx).n += 16
            Sp("sFW", idx).mark(idx)

        with contextlib.ExitStack() as ph_a:
            psS = ph_a.enter_context(nc.psum_tensor("psS", [128, 2048], F32))
            wst = ph_a.enter_context(nc.sbuf_tensor("wst", [128, 1024], F32))
            wpq = ph_a.enter_context(nc.sbuf_tensor("wpq", [128, 2 * 1024], BF16))
            wpk = ph_a.enter_context(nc.sbuf_tensor("wpk", [128, 2 * 1024], BF16))
            wpv = ph_a.enter_context(nc.sbuf_tensor("wpv", [128, 2 * 1024], BF16))
            wcf = ph_a.enter_context(nc.sbuf_tensor("wcf", [128, 2 * 1024], F32))
            wcb = ph_a.enter_context(nc.sbuf_tensor("wcb", [128, 2 * 1024], BF16))
            xt = ph_a.enter_context(nc.sbuf_tensor("xt", [128, 2 * 1024], F32))
            qt = ph_a.enter_context(nc.sbuf_tensor("qt", [128, T], BF16))
            kt = ph_a.enter_context(nc.sbuf_tensor("kt", [128, T], BF16))
            vt = ph_a.enter_context(nc.sbuf_tensor("vt", [128, NT * 129], BF16))
            es = ph_a.enter_context(nc.sbuf_tensor("es", [128, 2 * 2048], BF16))
            nd = ph_a.enter_context(nc.sbuf_tensor("nd", [128, 2048], F32))
            astg = ph_a.enter_context(nc.sbuf_tensor("astg", [128, 2 * 1024], F32))


            P.emit("sync", lambda En: En.dma_start(out=idf[:], in_=idf_d)
                   .then_inc(S("sMISC").h, 16))
            S("sMISC").n += 16
            P.emit("sync", lambda En: En.dma_start(
                out=astg[:, 0:64].rearrange("p (i e) -> p i e", e=8),
                in_=rt_d.rearrange("(i p) e -> p i e", p=128))
                .then_inc(S("sMISC").h, 16))
            S("sMISC").n += 16
            P.wait("vector", S("sMISC"), 32)
            P.emit("vector", lambda En: En.tensor_copy(rt_bf[:], astg[:, 0:64])
                   .then_inc(S("sMISC").h, 1))
            S("sMISC").n += 1
            P.emit("vector", lambda En: En.memset(
                vt[:].rearrange("p (j c) -> p j c", c=129)[:, :, 64:65], 1.0)
                .then_inc(S("sONE").h, 1))
            S("sONE").n += 1

            def wpair_load_cast(p):
                b = p % 2
                for wi, wsrc in enumerate((wq_d, wk_d, wv_d)):
                    prev = 3 * p + wi - 1
                    P.wait("gpsimd", S("sWC"),
                           S("sWC").at(prev) if prev >= 0 else 0)
                    P.emit("gpsimd", lambda En, i=wsrc, p=p: En.dma_start(
                        out=wst[:].rearrange("p (i m) -> p i m", m=128),
                        in_=i[:, p * 128:(p + 1) * 128]
                            .rearrange("(i p) m -> p i m", p=128))
                        .then_inc(S("sWP").h, 16))
                    S("sWP").n += 16
                    S("sWP").mark(3 * p + wi)
                    dstt = (wpq, wpk, wpv)[wi]
                    P.wait("vector", S("sWP"), S("sWP").at(3 * p + wi))
                    if p >= 2 and wi == 0:
                        P.wait("vector", S("sQKVm"),
                               S("sQKVm").at(("pairdone", p - 2)))
                    P.emit("vector", lambda En, o=dstt, b=b: En.tensor_copy(
                        o[:, b * 1024:(b + 1) * 1024], wst[:])
                        .then_inc(S("sWC").h, 1))
                    S("sWC").n += 1
                    S("sWC").mark(3 * p + wi)
                S("sWC").mark(("pair", p))

            wpair_load_cast(0)

            def ln_stage1(j):
                b = j % 2
                s0 = b * 8
                xtb = xt[:, b * 1024:(b + 1) * 1024]
                P.wait("sync", S("sTR"), S("sTR").at(j - 2) if j >= 2 else 0)
                P.emit("sync", lambda En, j=j, o=xtb: En.dma_start(
                    out=o, in_=x_d[j * 128:(j + 1) * 128, :])
                    .then_inc(Sp("sX", j).h, 16))
                Sp("sX", j).n += 16
                Sp("sX", j).mark(j)
                P.wait("vector", Sp("sX", j), Sp("sX", j).at(j))
                P.emit("vector", lambda En, i=xtb, s0=s0: En.reduce_sum(
                    stat[:, s0:s0 + 1], i, axis=AX.X).then_inc(S("sLNa").h, 1))
                S("sLNa").n += 1
                P.wait("vector", S("sLNa"), S("sLNa").n)
                P.emit("vector", lambda En, s0=s0: En.tensor_scalar(
                    stat[:, s0 + 1:s0 + 2], stat[:, s0:s0 + 1], 1.0 / E, None,
                    ALU.mult).then_inc(S("sLNa").h, 1))
                S("sLNa").n += 1
                P.wait("vector", S("sLNa"), S("sLNa").n)
                P.emit("vector", lambda En, i=xtb, s0=s0: En.tensor_scalar(
                    i, i, stat[:, s0 + 1:s0 + 2], None, ALU.subtract)
                    .then_inc(S("sLNa").h, 1))
                S("sLNa").n += 1
                S("sLNa").mark(("xc", j))
                P.wait("scalar", S("sLNa"), S("sLNa").at(("xc", j)))
                if j >= 2:
                    P.wait("scalar", S("sLNa"), S("sLNa").at(("ve", j - 2)))
                if j == 0:
                    P.wait("scalar", S("sMISC"), 33)
                P.emit("scalar", lambda En, i=xtb, b=b, s0=s0: En.activation(
                    astg[:, b * 1024:(b + 1) * 1024], i, AF.Square,
                    accum_out=stat[:, s0 + 2:s0 + 3]).then_inc(S("sLNb").h, 1))
                S("sLNb").n += 1
                S("sLNb").mark(("sq", j))

            def ln_stage2(j):
                b = j % 2
                s0 = b * 8
                xtb = xt[:, b * 1024:(b + 1) * 1024]
                P.wait("vector", S("sLNb"), S("sLNb").at(("sq", j)))
                P.emit("vector", lambda En, s0=s0: En.tensor_scalar(
                    stat[:, s0 + 3:s0 + 4], stat[:, s0 + 2:s0 + 3], 1.0 / E,
                    EPS, ALU.mult, ALU.add).then_inc(S("sLNa").h, 1))
                S("sLNa").n += 1
                S("sLNa").mark(("ve", j))
                P.wait("scalar", S("sLNa"), S("sLNa").at(("ve", j)))
                if j >= 2:
                    P.wait("scalar", S("sLNc"), S("sLNc").at(j - 2))
                P.emit("scalar", lambda En, s0=s0: En.activation(
                    stat[:, s0 + 4:s0 + 5], stat[:, s0 + 3:s0 + 4], AF.Ln)
                    .then_inc(S("sLNb").h, 1))
                S("sLNb").n += 1
                P.wait("scalar", S("sLNb"), S("sLNb").n)
                P.emit("scalar", lambda En, s0=s0: En.activation(
                    stat[:, s0 + 5:s0 + 6], stat[:, s0 + 4:s0 + 5], AF.Exp,
                    scale=-0.5).then_inc(S("sLNb").h, 1))
                S("sLNb").n += 1
                S("sLNb").mark(("rs", j))
                P.wait("vector", S("sLNb"), S("sLNb").at(("rs", j)))
                P.emit("vector", lambda En, i=xtb, s0=s0: En.tensor_scalar(
                    i, i, stat[:, s0 + 5:s0 + 6], None, ALU.mult)
                    .then_inc(S("sLNc").h, 1))
                S("sLNc").n += 1
                S("sLNc").mark(j)
                P.wait("tensor", S("sLNc"), S("sLNc").at(j))
                if j >= 2:
                    P.wait("tensor", S("sXNT"), S("sXNT").at(j - 2))
                if j == 0:
                    P.wait("tensor", S("sMISC"), 32)
                for c in range(8):
                    TRN(psS[:, b * 1024 + c * 128:b * 1024 + (c + 1) * 128],
                        xtb[:, c * 128:(c + 1) * 128],
                        128, sem=S("sTR") if c == 7 else None)
                S("sTR").mark(j)
                P.wait("vector", S("sTR"), S("sTR").at(j))
                P.emit("vector", lambda En, j=j, b=b: En.tensor_copy(
                    xnt[:].rearrange("p (i t) -> p i t", t=T)
                        [:, :, j * 128:(j + 1) * 128],
                    psS[:, b * 1024:(b + 1) * 1024]
                        .rearrange("p (i t) -> p i t", t=128))
                    .then_inc(S("sXNT").h, 1))
                S("sXNT").n += 1
                S("sXNT").mark(j)

            for j in range(NT + 1):
                if j < NT:
                    ln_stage1(j)
                if j >= 1:
                    ln_stage2(j - 1)

            if stage == "ln":
                P.wait("sync", S("sXNT"), S("sXNT").at(NT - 1))
                P.emit("sync", lambda En: En.dma_start(out=dbg_d, in_=xnt[:])
                       .then_inc(S("sMISC").h, 16))
                S("sMISC").n += 16
                P.wait("sync", S("sMISC"), S("sMISC").n)
                P.flush(block)
                return nc

            pieces = []
            for e in range(NEXP):
                g, u, d = psrc(e)
                for r in range(4):
                    pieces.append((g[256 * r:256 * (r + 1), :],
                                   gbf_d[e][:, r * 1024:(r + 1) * 1024]
                                   .rearrange("p (i c) -> p i c", c=512)))
                for r in range(4):
                    pieces.append((u[256 * r:256 * (r + 1), :],
                                   ubf_d[e][:, r * 1024:(r + 1) * 1024]
                                   .rearrange("p (i c) -> p i c", c=512)))
                for ch in range(2):
                    for r in range(2):
                        pieces.append((d[256 * r:256 * (r + 1),
                                         512 * ch:512 * (ch + 1)],
                                       dbf_d[e].rearrange("p (k h) -> p k h", h=1024)
                                       [:, 2 * r:2 * r + 2, ch * 512:(ch + 1) * 512]))
            NPC = len(pieces)

            def precast_step(k):
                b = k % 2
                src, dst = pieces[k]
                fbuf = wcf[:, b * 1024:(b + 1) * 1024]
                bbuf = wcb[:, b * 1024:(b + 1) * 1024]
                P.wait("gpsimd", S("sWCc"), S("sWCc").at(k - 2) if k >= 2 else 0)
                P.emit("gpsimd", lambda En, o=fbuf, i=src: En.dma_start(
                    out=o.rearrange("p (i c) -> p i c", c=512),
                    in_=i.rearrange("(i p) c -> p i c", p=128))
                    .then_inc(Sp("sWCf", k).h, 16))
                Sp("sWCf", k).n += 16
                Sp("sWCf", k).mark(k)
                P.wait("vector", Sp("sWCf", k), Sp("sWCf", k).at(k))
                if k >= 2:
                    P.wait("vector", Sp("sWCo", k), Sp("sWCo", k).at(k - 2))
                P.emit("vector", lambda En, o=bbuf, i=fbuf: En.tensor_copy(o, i)
                       .then_inc(S("sWCc").h, 1))
                S("sWCc").n += 1
                S("sWCc").mark(k)
                P.wait("gpsimd", S("sWCc"), S("sWCc").at(k))
                P.emit("gpsimd", lambda En, o=dst, i=bbuf, kk=k: En.dma_start(
                    out=o, in_=i.rearrange("p (i c) -> p i c", c=512))
                    .then_inc(Sp("sWCo", kk).h, 16))
                Sp("sWCo", k).n += 16
                Sp("sWCo", k).mark(k)

            precast_iter = iter(range(NPC))

            NC2 = T // 512

            def att_prev(p, c, j, k):
                g = (p * NC2 + c) * KT + j - k
                if g < 0:
                    return None
                pp, r = divmod(g, NC2 * KT)
                cc, jj = divmod(r, KT)
                return (pp, cc, jj)

            def pair_prev(p, c, m, k):
                g = (p * NC2 + c) * (KT // 2) + m - k
                if g < 0:
                    return None
                pp, r = divmod(g, NC2 * (KT // 2))
                cc, mm = divmod(r, KT // 2)
                return (pp, cc, mm)

            def S_emit(p, c, j):
                q0 = c * 512
                so = (j % 2) * 1024
                prev = pair_prev(p, c, j // 2, 1)
                if prev is not None:
                    P.wait("tensor", S("sEXP"), S("sEXP").at(prev))
                if c == 0 and j == 0:
                    P.wait("tensor", S("sQKVe"), S("sQKVe").at(("allqkv", p)))
                for hi, ktb in ((0, 0), (1, 64)):
                    MM(psS[:, so + hi * 512:so + (hi + 1) * 512],
                       kt[ktb:ktb + 64, j * 128:(j + 1) * 128],
                       qt[ktb:ktb + 64, q0:q0 + 512],
                       start=True, stop=True,
                       sem=S("sS") if hi == 1 else None)
                S("sS").mark((p, c, j))

            def EXP_emit(p, c, m):
                P.wait("scalar", S("sS"), S("sS").at((p, c, 2 * m + 1)))
                prev = pair_prev(p, c, m, 2)
                if prev is not None:
                    pp, cc, mm = prev
                    P.wait("scalar", S("sPVA"),
                           S("sPVA").at((pp, cc, 2 * mm + 1)))
                P.emit("scalar", lambda En, b2=m % 2: En.activation(
                    es[:, b2 * 2048:(b2 + 1) * 2048], psS[:, 0:2048],
                    AF.Exp, scale=0.125).then_inc(S("sEXP").h, 1))
                S("sEXP").n += 1
                S("sEXP").mark((p, c, m))

            def PV_emit(p, c, jj):
                P.wait("tensor", S("sEXP"), S("sEXP").at((p, c, jj // 2)))
                if jj == 0:
                    P.wait("tensor", S("sONE"), 1)
                    P.wait("tensor", S("sVTe"), S("sVTe").at((p, NT - 1)))
                    if c >= 1:
                        P.wait("tensor", S("sNDA"), S("sNDA").at((p, c - 1)))
                    elif p >= 1:
                        P.wait("tensor", S("sNDA"),
                               S("sNDA").at((p - 1, NC2 - 1)))
                eo = ((jj // 2) % 2) * 2048 + (jj % 2) * 1024
                for hi, voff in ((0, 0), (1, 64)):
                    MM(ps2[0:65, hi * 512:(hi + 1) * 512],
                       vt[:, jj * 129 + voff:jj * 129 + voff + 65],
                       es[:, eo + hi * 512:eo + (hi + 1) * 512],
                       start=(jj == 0), stop=(jj == KT - 1),
                       sem=S("sPVA") if hi == 1 else None)
                S("sPVA").mark((p, c, jj))

            for p in range(8):
                b = p % 2
                if p + 1 < 8:
                    wpair_load_cast(p + 1)
                for (wbuf, dst, pi) in ((wpq, qt, 0), (wpk, kt, 1)):
                    for np2 in range(NQC // 2):
                        n0 = 2 * np2
                        if np2 >= 1:
                            P.wait("tensor", S("sQKVe"),
                                   S("sQKVe").at((p, pi, n0 - 1)))
                        elif pi == 1:
                            P.wait("tensor", S("sQKVe"),
                                   S("sQKVe").at((p, 0, NQC - 1)))
                        elif p >= 1:
                            P.wait("tensor", S("sNDA"),
                                   S("sNDA").at((p - 1, NC2 - 1)))
                        if np2 == 0:
                            P.wait("tensor", S("sWC"), S("sWC").at(("pair", p)))
                        if p == 0:
                            P.wait("tensor", S("sXNT"), S("sXNT").at(
                                min(NT - 1, (np2 + 1) * 8 - 1)))
                        for c in range(NE):
                            for nh in range(2):
                                n = n0 + nh
                                MM(ps2[:, nh * 512:(nh + 1) * 512],
                                   wbuf[:, b * 1024 + c * 128:
                                        b * 1024 + (c + 1) * 128],
                                   xnt[:, c * T + n * 512:c * T + (n + 1) * 512],
                                   start=(c == 0), stop=(c == NE - 1),
                                   sem=S("sQKVm")
                                   if (c == NE - 1 and nh == 1) else None)
                        S("sQKVm").mark((p, pi, n0))
                        P.wait("vector", S("sQKVm"), S("sQKVm").at((p, pi, n0)))
                        if p >= 1 and np2 == 0 and pi == 0:
                            P.wait("vector", S("sS"), S("sS").at(("pair", p - 1)))
                        for nh in range(2):
                            n = n0 + nh
                            P.emit("vector", lambda En, o=dst, n=n, nh=nh:
                                   En.tensor_copy(
                                o[:, n * 512:(n + 1) * 512],
                                ps2[:, nh * 512:(nh + 1) * 512])
                                .then_inc(S("sQKVe").h, 1))
                            S("sQKVe").n += 1
                            S("sQKVe").mark((p, pi, n))
                for n in range(NQC):
                    half = n % 2
                    pslice = ps2[:, half * 512:(half + 1) * 512]
                    P.wait("tensor", S("sQKVe"),
                           S("sQKVe").at((p, "v", n - 2)) if n >= 2 else
                           S("sQKVe").at((p, 1, NQC - 1)))
                    for c in range(NE):
                        MM(pslice,
                           wpv[:, b * 1024 + c * 128:b * 1024 + (c + 1) * 128],
                           xnt[:, c * T + n * 512:c * T + (n + 1) * 512],
                           start=(c == 0), stop=(c == NE - 1),
                           sem=S("sQKVm") if c == NE - 1 else None)
                    S("sQKVm").mark((p, "v", n))
                    P.wait("vector", S("sQKVm"), S("sQKVm").at((p, "v", n)))
                    vh = n % 2
                    vtf = xt[:, vh * 1024:vh * 1024 + 512]
                    if n >= 2:
                        P.wait("vector", S("sVT"), S("sVT").at((p, (n - 2) * 4 + 3)))
                    elif p >= 1:
                        P.wait("vector", S("sVT"), S("sVT").at(("pair", p - 1)))
                    else:
                        P.wait("vector", S("sLNc"), S("sLNc").n)
                    P.emit("vector", lambda En, o=vtf, i=pslice: En.tensor_copy(o, i)
                           .then_inc(S("sQKVe").h, 1))
                    S("sQKVe").n += 1
                    S("sQKVe").mark((p, "v", n))
                    tcol = (n % 2) * 512
                    P.wait("tensor", S("sQKVe"), S("sQKVe").at((p, "v", n)))
                    if n >= 2:
                        P.wait("tensor", S("sVTe"), S("sVTe").at((p, (n - 2) * 4 + 3)))
                    else:
                        P.wait("tensor", S("sQKVe"), S("sQKVe").at((p, 1, NQC - 1)))
                        if p >= 1 and n == 0:
                            P.wait("tensor", S("sDIV"),
                                   S("sDIV").at((p - 1, NC2 - 1)))
                    for k in range(4):
                        tt = n * 4 + k
                        TRN(ps3[:, tcol + k * 128:tcol + (k + 1) * 128],
                            vtf[:, k * 128:(k + 1) * 128],
                            128, sem=S("sVT") if k == 3 else None)
                        S("sVT").mark((p, tt))
                    P.wait("vector", S("sVT"), S("sVT").at((p, n * 4 + 3)))
                    for k in range(4):
                        tt = n * 4 + k
                        col = tcol + k * 128
                        P.emit("vector", lambda En, tt=tt, col=col:
                               En.tensor_copy(vt[:, tt * 129:tt * 129 + 64],
                                              ps3[:, col:col + 64]))
                        P.emit("vector", lambda En, tt=tt, col=col:
                               En.tensor_copy(vt[:, tt * 129 + 65:tt * 129 + 129],
                                              ps3[:, col + 64:col + 128])
                               .then_inc(S("sVTe").h, 1))
                        S("sVTe").n += 1
                        S("sVTe").mark((p, tt))
                S("sVT").mark(("pair", p))
                S("sQKVm").mark(("pairdone", p))
                S("sQKVe").mark(("allqkv", p))

                if stage == "qkv" and p == 0:
                    P.wait("sync", S("sQKVe"), S("sQKVe").at(("allqkv", 0)))
                    P.wait("sync", S("sVTe"), S("sVTe").at((0, NT - 1)))
                    P.emit("sync", lambda En: En.dma_start(
                        out=dbg_d[:, 0:T], in_=qt[:]).then_inc(S("sMISC").h, 16))
                    S("sMISC").n += 16
                    P.emit("sync", lambda En: En.dma_start(
                        out=dbg_d[:, T:2 * T], in_=kt[:]).then_inc(S("sMISC").h, 16))
                    S("sMISC").n += 16
                    P.emit("sync", lambda En: En.dma_start(
                        out=dbg_d[:, 2 * T:2 * T + NT * 129], in_=vt[:])
                        .then_inc(S("sMISC").h, 16))
                    S("sMISC").n += 16
                    P.wait("sync", S("sMISC"), S("sMISC").n)
                    P.flush(block)
                    return nc

                for c in range(NC2):
                    pre = 2 if c > 0 else 0
                    for j in range(KT):
                        if j >= pre:
                            S_emit(p, c, j)
                        if j % 2 == 1:
                            EXP_emit(p, c, j // 2)
                        if j >= 2:
                            PV_emit(p, c, j - 2)
                    PV_emit(p, c, KT - 2)
                    PV_emit(p, c, KT - 1)
                    if c + 1 < NC2:
                        S_emit(p, c + 1, 0)
                        S_emit(p, c + 1, 1)
                    gidx = p * NC2 + c
                    cs = gidx % 4
                    stg = astg[:, cs * 512:(cs + 1) * 512]
                    P.wait("vector", S("sPVA"), S("sPVA").at((p, c, KT - 1)))
                    P.emit("vector", lambda En: En.tensor_copy(
                        nd[0:65, 0:1024], ps2[0:65, 0:1024])
                        .then_inc(S("sNDA").h, 1))
                    S("sNDA").n += 1
                    S("sNDA").mark((p, c))
                    P.wait("tensor", S("sNDA"), S("sNDA").at((p, c)))
                    if c == 0:
                        P.wait("tensor", S("sVTe"), S("sVTe").at((p, NT - 1)))
                        if p >= 1:
                            P.wait("tensor", S("sDIV"),
                                   S("sDIV").at((p - 1, NC2 - 1)))
                    else:
                        P.wait("tensor", S("sDIV"), S("sDIV").at((p, c - 1)))
                    for hi, ndo in ((0, 0), (1, 512)):
                        for t in range(4):
                            TRN(ps3[:, ndo + t * 128:ndo + t * 128 + 65],
                                nd[0:65, ndo + t * 128:ndo + (t + 1) * 128],
                                65, sem=S("sTRA") if (hi == 1 and t == 3)
                                else None)
                    S("sTRA").mark((p, c))
                    P.wait("vector", S("sTRA"), S("sTRA").at((p, c)))
                    P.emit("vector", lambda En: En.tensor_copy(
                        nd[:, 0:1024], ps3[:, 0:1024])
                        .then_inc(S("sDIV").h, 1))
                    S("sDIV").n += 1
                    S("sDIV").mark((p, c))
                    for (hh, ndo) in (("A", 0), ("B", 512)):
                        dcol = 64 if hh == "A" else 0
                        ncol0 = 0 if hh == "A" else 1
                        scol = 0 if hh == "A" else 64
                        if hh == "A":
                            P.wait("vector", S("sDIV"), S("sDIV").at((p, c)))
                        if hh == "A" and gidx >= 4:
                            P.wait("vector", SST(gidx), SST(gidx).at(gidx - 4))
                        for t in range(4):
                            if not (hh == "A" and t == 0):
                                P.wait("vector", S("sDV2"), S("sDV2").n)
                            P.emit("vector", lambda En, ndo=ndo, t=t, dcol=dcol:
                                   En.reciprocal(
                                stat[:, 16:17],
                                nd[:, ndo + t * 128 + dcol:
                                   ndo + t * 128 + dcol + 1])
                                .then_inc(S("sSMv").h, 1))
                            S("sSMv").n += 1
                            P.wait("vector", S("sSMv"), S("sSMv").n)
                            P.emit("vector", lambda En, ndo=ndo, t=t, n0=ncol0,
                                   sc=scol, stg=stg: En.tensor_scalar(
                                stg[:, t * 128 + sc:t * 128 + sc + 64],
                                nd[:, ndo + t * 128 + n0:ndo + t * 128 + n0 + 64],
                                stat[:, 16:17], None, ALU.mult)
                                .then_inc(S("sDV2").h, 1))
                            S("sDV2").n += 1
                    S("sDV2").mark((p, c))
                    P.wait("sync", S("sDV2"), S("sDV2").at((p, c)))
                    P.emit("sync", lambda En, p=p, c=c, stg=stg,
                           sh=SST(gidx).h: En.dma_start(
                        out=attn_d.rearrange("(J p) e -> p J e", p=128)
                            [:, c * 4:(c + 1) * 4, p * 128:(p + 1) * 128],
                        in_=stg.rearrange("p (J e) -> p J e", e=128))
                        .then_inc(sh, 16))
                    SST(gidx).n += 16
                    SST(gidx).mark(gidx)
                    for _ in range(2):
                        k = next(precast_iter, None)
                        if k is not None:
                            precast_step(k)
                S("sEXP").mark(("pair", p))
                S("sS").mark(("pair", p))
                if p == 3:
                    ffn_wload(0, 0)
                elif p == 5:
                    ffn_wload(1, 1)
            for k in precast_iter:
                precast_step(k)
            for i2 in range(4):
                SST(i2).mark("all")
            for i2 in range(2):
                Sp("sWCo", i2).mark("all")
            S("sDIV").mark("attn_done")
            S("sEXP").mark("attn_done")

            if stage == "attn":
                for i2 in range(4):
                    P.wait("sync", SST(i2), SST(i2).at("all"))
                for i2 in range(2):
                    P.wait("sync", Sp("sWCo", i2), Sp("sWCo", i2).at("all"))
                P.flush(block)
                return nc

            P.flush(block)

        with contextlib.ExitStack() as ph_f:
            psD = ph_f.enter_context(nc.psum_tensor("psD", [128, 2048], F32))
            acc = ph_f.enter_context(nc.sbuf_tensor("acc", [128, 8 * 1024], F32))
            m_sb = ph_f.enter_context(nc.sbuf_tensor("m_sb", [128, 2 * 4096], BF16))
            xres = ph_f.enter_context(nc.sbuf_tensor("xres", [128, 2 * 1024], F32))
            ares = ph_f.enter_context(nc.sbuf_tensor("ares", [128, 2 * 1024], F32))
            sl = ph_f.enter_context(nc.sbuf_tensor("sl", [128, 1024], F32))

            def phase_gate(eng):
                for i2 in range(2):
                    P.wait(eng, Sp("sWCo", i2), Sp("sWCo", i2).at("all"))
                for i2 in range(4):
                    P.wait(eng, SST(i2), SST(i2).at("all"))
                P.wait(eng, S("sDIV"), S("sDIV").at("attn_done"))
                P.wait(eng, S("sEXP"), S("sEXP").at("attn_done"))
                P.wait(eng, S("sWC"), S("sWC").n)
                P.wait(eng, S("sVTe"), S("sVTe").n)
                P.wait(eng, S("sQKVe"), S("sQKVe").n)
            for eng in ("tensor", "vector", "scalar", "sync", "gpsimd"):
                phase_gate(eng)
            for g in range(NT // 4):
                cb = (g % 2) * 1024
                if g >= 2:
                    P.wait("tensor", S("sRTe"), S("sRTe").at(g - 2))
                for t4 in range(4):
                    tt = g * 4 + t4
                    for i in range(NE):
                        MM(psD[:, cb + t4 * 128:cb + t4 * 128 + 8],
                           xnt[:, i * T + tt * 128:i * T + (tt + 1) * 128],
                           rt_bf[:, i * 8:(i + 1) * 8],
                           start=(i == 0), stop=(i == NE - 1),
                           sem=S("sRT") if (t4 == 3 and i == NE - 1) else None)
                S("sRT").mark(g)
                P.wait("vector", S("sRT"), S("sRT").at(g))
                P.emit("vector", lambda En, cb=cb, g=g: En.tensor_copy(
                    logit[:, g * 32:(g + 1) * 32].rearrange("p (t r) -> p t r", r=8),
                    psD[:].rearrange("p (t r) -> p t r", r=128)
                        [:, 8 * (cb // 1024):8 * (cb // 1024) + 4, 0:8])
                    .then_inc(S("sRTe").h, 1))
                S("sRTe").n += 1
                S("sRTe").mark(g)
            S("sRTe").mark("all")
            for tt in range(NT):
                L = logit[:, tt * 8:(tt + 1) * 8]
                P.wait("vector", S("sRTe"), S("sRTe").at(tt // 4))
                c0, c1, c2, c3, c4 = (stat[:, 9:10], stat[:, 10:11], stat[:, 11:12],
                                      stat[:, 12:13], stat[:, 13:14])
                pbuf, ge1, msk, ge2 = (st8[:, 0:8], st8[:, 8:16],
                                       st8[:, 16:24], st8[:, 24:32])
                P.emit("vector", lambda En, L=L, o=c0: En.reduce_max(
                    o, L, axis=AX.X, negate=True).then_inc(S("sSMv").h, 1))
                S("sSMv").n += 1
                P.emit("vector", lambda En, L=L, o=c1: En.reduce_max(
                    o, L, axis=AX.X).then_inc(S("sSMv").h, 1))
                S("sSMv").n += 1
                P.wait("scalar", S("sSMv"), S("sSMv").n)
                P.emit("scalar", lambda En, L=L, o=pbuf, bb=c0, a=c2: En.activation(
                    o, L, AF.Exp, bias=bb, accum_out=a).then_inc(S("sSMs").h, 1))
                S("sSMs").n += 1
                S("sSMs").mark(("exp", tt))
                P.wait("vector", S("sSMs"), S("sSMs").at(("exp", tt)))
                P.emit("vector", lambda En, L=L, o=ge1, m=c1: En.tensor_single_scalar(
                    o, L, m, ALU.is_ge).then_inc(S("sSMv").h, 1))
                S("sSMv").n += 1
                P.wait("vector", S("sSMv"), S("sSMv").n)
                P.emit("vector", lambda En, L=L, o=msk, g1=ge1: En.scalar_tensor_tensor(
                    o, g1, -1e30, L, ALU.mult, ALU.add).then_inc(S("sSMv").h, 1))
                S("sSMv").n += 1
                P.wait("vector", S("sSMv"), S("sSMv").n)
                P.emit("vector", lambda En, o=c3, m=msk: En.reduce_max(
                    o, m, axis=AX.X).then_inc(S("sSMv").h, 1))
                S("sSMv").n += 1
                P.wait("vector", S("sSMv"), S("sSMv").n)
                P.emit("vector", lambda En, L=L, o=ge2, m=c3: En.tensor_single_scalar(
                    o, L, m, ALU.is_ge).then_inc(S("sSMv").h, 1))
                S("sSMv").n += 1
                P.emit("vector", lambda En, o=c4, i=c2: En.reciprocal(o, i)
                       .then_inc(S("sSMv").h, 1))
                S("sSMv").n += 1
                P.wait("vector", S("sSMv"), S("sSMv").n)
                P.emit("vector", lambda En, o=pbuf, r=c4: En.tensor_scalar(
                    o, o, r, None, ALU.mult).then_inc(S("sSMv").h, 1))
                S("sSMv").n += 1
                P.wait("vector", S("sSMv"), S("sSMv").n)
                P.emit("vector", lambda En, o=coef[:, tt * 8:(tt + 1) * 8],
                       a=pbuf, g2=ge2: En.tensor_mul(o, a, g2)
                       .then_inc(S("sSMv").h, 1))
                S("sSMv").n += 1
            S("sSMv").mark("coef_all")

            for cch in range(NCH):
                for t in range(8):
                    tb = t % 2
                    gt = cch * 8 + t
                    if t >= 2:
                        P.wait("sync", S("sINIT"), S("sINIT").at((cch, t - 2)))
                    elif cch >= 1:
                        P.wait("sync", S("sINIT"), S("sINIT").at((cch - 1, 6 + t)))
                    for i2 in range(4):
                        P.wait("sync", SST(i2), SST(i2).at("all"))
                    P.emit("sync", lambda En, gt=gt, tb=tb, sh=Sp("sXR", t).h:
                           En.dma_start(
                        out=xres[:, tb * 1024:(tb + 1) * 1024],
                        in_=x_d[gt * 128:(gt + 1) * 128, :]).then_inc(sh, 16))
                    Sp("sXR", t).n += 16
                    P.emit("sync", lambda En, gt=gt, tb=tb, sh=Sp("sXR", t).h:
                           En.dma_start(
                        out=ares[:, tb * 1024:(tb + 1) * 1024],
                        in_=attn_d[gt * 128:(gt + 1) * 128, :]).then_inc(sh, 16))
                    Sp("sXR", t).n += 16
                    Sp("sXR", t).mark((cch, t))
                    P.wait("vector", Sp("sXR", t), Sp("sXR", t).at((cch, t)))
                    if cch >= 1:
                        P.wait("vector", S("sOUT"), S("sOUT").at(("chunk", cch - 1)))
                    P.emit("vector", lambda En, t=t, tb=tb: En.tensor_add(
                        acc[:, t * 1024:(t + 1) * 1024],
                        xres[:, tb * 1024:(tb + 1) * 1024],
                        ares[:, tb * 1024:(tb + 1) * 1024]).then_inc(S("sINIT").h, 1))
                    S("sINIT").n += 1
                    S("sINIT").mark((cch, t))

                for e in range(NEXP):
                    idx = cch * NEXP + e
                    eb = idx % 2
                    me = idx % 2
                    for k in range(4):
                        for (wsrc, psb, nm) in ((wg, ps2, "g"), (wu, ps3, "u")):
                            if nm == "g":
                                if k >= 1:
                                    P.wait("tensor", S("sSL"), S("sSL").at((idx, k - 1)))
                                elif idx >= 1:
                                    P.wait("tensor", S("sSL"), S("sSL").at((idx - 1, 3)))
                                elif stage == "full":
                                    pass
                            else:
                                if k >= 1:
                                    P.wait("tensor", S("sMUL"),
                                           S("sMUL").at((idx, k - 1)))
                                elif idx >= 1:
                                    P.wait("tensor", S("sMUL"),
                                           S("sMUL").at((idx - 1, 3)))
                            if k == 0 and nm == "g":
                                P.wait("tensor", Sp("sFW", idx), Sp("sFW", idx).at(idx))
                            for i in range(NE):
                                for hf in range(2):
                                    MM(psb[:, hf * 512:(hf + 1) * 512],
                                       wsrc[:, eb * 4096 + i * 512 + k * 128:
                                            eb * 4096 + i * 512 + (k + 1) * 128],
                                       xnt[:, i * T + cch * 1024 + hf * 512:
                                           i * T + cch * 1024 + (hf + 1) * 512],
                                       start=(i == 0), stop=(i == NE - 1),
                                       sem=S("sTEN") if (i == NE - 1 and hf == 1)
                                       else None)
                            S("sTEN").mark((idx, nm, k))
                        P.wait("scalar", S("sTEN"), S("sTEN").at((idx, "g", k)))
                        if k >= 1:
                            P.wait("scalar", S("sMUL"), S("sMUL").at((idx, k - 1)))
                        elif idx >= 1:
                            P.wait("scalar", S("sMUL"), S("sMUL").at((idx - 1, 3)))
                        P.emit("scalar", lambda En: En.activation(
                            sl[:], ps2[:, 0:1024], AF.Silu).then_inc(S("sSL").h, 1))
                        S("sSL").n += 1
                        S("sSL").mark((idx, k))
                        P.wait("vector", S("sSL"), S("sSL").at((idx, k)))
                        P.wait("vector", S("sTEN"), S("sTEN").at((idx, "u", k)))
                        if idx >= 2 and k == 0:
                            P.wait("vector", S("sTEN"), S("sTEN").at(("edone", idx - 2)))
                        P.emit("vector", lambda En, me=me, k=k: En.tensor_mul(
                            m_sb[:, me * 4096 + k * 1024:me * 4096 + (k + 1) * 1024],
                            sl[:], ps3[:, 0:1024]).then_inc(S("sMUL").h, 1))
                        S("sMUL").n += 1
                        S("sMUL").mark((idx, k))
                    for t in range(8):
                        pcol = (t % 2) * 1024
                        if t >= 2:
                            P.wait("tensor", S("sACC"), S("sACC").at((idx, t - 2)))
                        elif idx >= 1:
                            P.wait("tensor", S("sACC"), S("sACC").at((idx - 1, 6 + t)))
                        else:
                            P.wait("tensor", S("sRTe"), S("sRTe").at("all"))
                        if t == 0:
                            P.wait("tensor", S("sMUL"), S("sMUL").at((idx, 3)))
                        for k in range(4):
                            for hf in range(2):
                                MM(psD[:, pcol + hf * 512:pcol + (hf + 1) * 512],
                                   m_sb[:, me * 4096 + k * 1024 + t * 128:
                                        me * 4096 + k * 1024 + (t + 1) * 128],
                                   wd[:, eb * 4096 + k * 1024 + hf * 512:
                                      eb * 4096 + k * 1024 + (hf + 1) * 512],
                                   start=(k == 0), stop=(k == 3),
                                   sem=S("sTEN") if (hf == 1 and k == 3) else None)
                        S("sTEN").mark((idx, "d", t))
                        P.wait("vector", S("sTEN"), S("sTEN").at((idx, "d", t)))
                        P.wait("vector", S("sINIT"), S("sINIT").at((cch, t)))
                        if e >= 2:
                            P.wait("vector", S("sSMv"), S("sSMv").at("coef_all"))
                        if e >= 1:
                            P.wait("vector", S("sACC"), S("sACC").at((idx - 1, t)))
                        gt = cch * 8 + t
                        if e < 2:
                            P.emit("vector", lambda En, pcol=pcol, t=t: En.tensor_add(
                                acc[:, t * 1024:(t + 1) * 1024],
                                acc[:, t * 1024:(t + 1) * 1024],
                                psD[:, pcol:pcol + 1024]).then_inc(S("sACC").h, 1))
                        else:
                            P.emit("vector", lambda En, pcol=pcol, t=t, gt=gt, e=e:
                                   En.scalar_tensor_tensor(
                                acc[:, t * 1024:(t + 1) * 1024],
                                psD[:, pcol:pcol + 1024],
                                coef[:, gt * 8 + (e - 2):gt * 8 + (e - 2) + 1],
                                acc[:, t * 1024:(t + 1) * 1024],
                                ALU.mult, ALU.add).then_inc(S("sACC").h, 1))
                        S("sACC").n += 1
                        S("sACC").mark((idx, t))
                    S("sTEN").mark(("edone", idx))
                    if idx + 2 < NCH * NEXP:
                        ffn_wload(idx + 2, (idx + 2) % NEXP)
                for t in range(8):
                    gt = cch * 8 + t
                    P.wait("sync", S("sACC"), S("sACC").at((cch * NEXP + NEXP - 1, t)))
                    P.emit("sync", lambda En, gt=gt, t=t: En.dma_start(
                        out=out_d[gt * 128:(gt + 1) * 128, :],
                        in_=acc[:, t * 1024:(t + 1) * 1024]).then_inc(S("sOUT").h, 16))
                    S("sOUT").n += 16
                    S("sOUT").mark((cch, t))
                S("sOUT").mark(("chunk", cch))
            P.wait("sync", S("sOUT"), S("sOUT").n)
            P.flush(block)

    return nc


_NC_CACHE = {}


def _get_nc(T):
    if T not in _NC_CACHE:
        _NC_CACHE[T] = build_nc(T)
    return _NC_CACHE[T]


def kernel(L0_attn_x, L0_attn_w_Q, L0_attn_w_K, L0_attn_w_V, L0_ffn_router,
           L0_ffn_experts_gate, L0_ffn_experts_up, L0_ffn_experts_down,
           L0_ffn_shared_gate, L0_ffn_shared_up, L0_ffn_shared_down):
    f32 = lambda a: np.ascontiguousarray(np.asarray(a, dtype=np.float32))
    x = f32(L0_attn_x)
    common = {
        "wq": f32(L0_attn_w_Q), "wk": f32(L0_attn_w_K), "wv": f32(L0_attn_w_V),
        "rt": f32(L0_ffn_router),
        "eg": f32(L0_ffn_experts_gate), "eu": f32(L0_ffn_experts_up),
        "ed": f32(L0_ffn_experts_down),
        "sg": f32(L0_ffn_shared_gate), "su": f32(L0_ffn_shared_up),
        "sd": f32(L0_ffn_shared_down),
        "idf": np.eye(128, dtype=np.float32),
    }
    nc = _get_nc(4096)
    in_maps = [dict(common, x=x[b]) for b in range(8)]
    res = run_bass_kernel_spmd(nc, in_maps, core_ids=list(range(8)))
    return np.stack([res.results[b]["out"] for b in range(8)], axis=0)


# version 11
# speedup vs baseline: 1.4559x; 1.4559x over previous
import contextlib
import numpy as np

import concourse.bass as bass
import concourse.mybir as mybir
import concourse.bass_utils as _bu
from concourse.bass_utils import run_bass_kernel_spmd

_orig_run_command = _bu.run_command


def _run_command_ldwopt(cmd, *a, **kw):
    cmd = [c.replace("--enable-ldw-opt=false", "--enable-ldw-opt=true")
           if isinstance(c, str) else c for c in cmd]
    return _orig_run_command(cmd, *a, **kw)


_bu.run_command = _run_command_ldwopt

F32 = mybir.dt.float32
BF16 = mybir.dt.bfloat16
AF = mybir.ActivationFunctionType
ALU = mybir.AluOpType
AX = mybir.AxisListType

E = 1024
NE = 8
NEXP = 10
EPS = 1e-5


class Sem:
    def __init__(self, stack, nc, name):
        self.h = stack.enter_context(nc.semaphore(name))
        self.n = 0
        self.hist = {}

    def mark(self, key):
        self.hist[key] = self.n

    def at(self, key):
        return self.hist[key]


class Prog:
    ENGS = ("sync", "gpsimd", "tensor", "vector", "scalar")

    def __init__(self):
        self.ops = {e: [] for e in self.ENGS}

    def emit(self, eng, fn):
        self.ops[eng].append(fn)

    def wait(self, eng, sem, val, future=False):
        assert future or val <= sem.n, f"wait {val} > produced {sem.n} ({eng})"
        if val <= 0:
            return
        self.emit(eng, lambda En, s=sem.h, v=val: En.wait_ge(s, v))

    def flush(self, block):
        for eng in self.ENGS:
            lst = self.ops[eng]
            if not lst:
                continue

            def body(En, lst=lst):
                for fn in lst:
                    fn(En)

            getattr(block, eng)(body)
        self.ops = {e: [] for e in self.ENGS}


def build_nc(T=4096, stage="full"):
    NT = T // 128
    NCH = T // 1024
    KT = NT
    NQC = T // 512
    nc = bass.Bass("TRN2", target_bir_lowering=False)

    x_d = nc.dram_tensor("x", [T, E], F32, kind="ExternalInput").ap()
    wq_d = nc.dram_tensor("wq", [E, E], F32, kind="ExternalInput").ap()
    wk_d = nc.dram_tensor("wk", [E, E], F32, kind="ExternalInput").ap()
    wv_d = nc.dram_tensor("wv", [E, E], F32, kind="ExternalInput").ap()
    rt_d = nc.dram_tensor("rt", [E, 8], F32, kind="ExternalInput").ap()
    eg_d = nc.dram_tensor("eg", [8, E, 512], F32, kind="ExternalInput").ap()
    eu_d = nc.dram_tensor("eu", [8, E, 512], F32, kind="ExternalInput").ap()
    ed_d = nc.dram_tensor("ed", [8, 512, E], F32, kind="ExternalInput").ap()
    sg_d = nc.dram_tensor("sg", [E, E], F32, kind="ExternalInput").ap()
    su_d = nc.dram_tensor("su", [E, E], F32, kind="ExternalInput").ap()
    sd_d = nc.dram_tensor("sd", [E, E], F32, kind="ExternalInput").ap()
    idf_d = nc.dram_tensor("idf", [128, 128], F32, kind="ExternalInput").ap()
    out_d = nc.dram_tensor("out", [T, E], F32, kind="ExternalOutput").ap()
    attn_d = nc.dram_tensor("attn_sc", [T, E], F32,
                            kind="Internal" if stage == "full"
                            else "ExternalOutput").ap()
    dbg_d = (nc.dram_tensor("dbg", [128, NE * T], BF16, kind="ExternalOutput").ap()
             if stage in ("ln", "qkv") else None)
    gbf_d = nc.dram_tensor("gbf", [NEXP, 128, 4096], BF16, kind="Internal").ap()
    ubf_d = nc.dram_tensor("ubf", [NEXP, 128, 4096], BF16, kind="Internal").ap()
    dbf_d = nc.dram_tensor("dbf", [NEXP, 128, 4096], BF16, kind="Internal").ap()

    def psrc(e):
        if e < 2:
            h = e * 512
            return (sg_d[:, h:h + 512], su_d[:, h:h + 512], sd_d[h:h + 512, :])
        return (eg_d[e - 2], eu_d[e - 2], ed_d[e - 2])

    stack = contextlib.ExitStack()
    with stack:
        xnt = stack.enter_context(nc.sbuf_tensor("xnt", [128, NE * T], BF16))
        idf = stack.enter_context(nc.sbuf_tensor("idf_sb", [128, 128], F32))
        stat = stack.enter_context(nc.sbuf_tensor("stat", [128, 32], F32))
        st8 = stack.enter_context(nc.sbuf_tensor("st8", [128, 64], F32))
        rt_bf = stack.enter_context(nc.sbuf_tensor("rt_bf", [128, 64], BF16))
        logit = stack.enter_context(nc.sbuf_tensor("logit", [128, NT * 8], F32))
        coef = stack.enter_context(nc.sbuf_tensor("coef", [128, NT * 8], F32))
        ps2 = stack.enter_context(nc.psum_tensor("ps2", [128, 1024], F32))
        ps3 = stack.enter_context(nc.psum_tensor("ps3", [128, 1024], F32))

        sm = {}
        for name in ("sXNT sLNa sLNb sLNc sTR sMISC "
                     "sWP sWC sQKVm sQKVe sVT sVTe sONE "
                     "sS sEXP sPVA sPVB sNDA sNDB sTRA sTRB sDIV sDV2 "
                     "sWCc sRT sRTe sSMv sSMs "
                     "sTEN sSL sMUL sACC sINIT sOUT").split():
            sm[name] = Sem(stack, nc, name)
        for name in ("sX", "sWCf", "sWCo", "sST", "sFW", "sXR"):
            sm[name + "0"] = Sem(stack, nc, name + "0")
            sm[name + "1"] = Sem(stack, nc, name + "1")
        sm["sST2"] = Sem(stack, nc, "sST2")
        sm["sST3"] = Sem(stack, nc, "sST3")
        S = lambda k: sm[k]
        Sp = lambda k, i: sm[k + str(i % 2)]
        SST = lambda g: sm["sST" + str(g % 4)]

        block = stack.enter_context(nc.Block())
        P = Prog()

        def MM(out_ap, lhsT, rhs, start, stop, sem=None):
            def f(En, o=out_ap, a=lhsT, r=rhs, s1=start, s2=stop, s=sem):
                ins = En.matmul(o, a, r, start=s1, stop=s2, skip_group_check=True)
                if s is not None:
                    ins.then_inc(s.h, 1)
            P.emit("tensor", f)
            if sem is not None:
                sem.n += 1

        def TRN(out_ap, in_ap, np_, sem=None):
            def f(En, o=out_ap, i=in_ap, idn=idf[0:np_, 0:np_], s=sem):
                ins = En.transpose(o, i, idn)
                if s is not None:
                    ins.then_inc(s.h, 1)
            P.emit("tensor", f)
            if sem is not None:
                sem.n += 1

        with contextlib.ExitStack() as ph_a:
            psS = [ph_a.enter_context(nc.psum_tensor(f"psS{i}", [128, 1024], F32))
                   for i in range(2)]
            wst = ph_a.enter_context(nc.sbuf_tensor("wst", [128, 1024], F32))
            wpq = ph_a.enter_context(nc.sbuf_tensor("wpq", [128, 2 * 1024], BF16))
            wpk = ph_a.enter_context(nc.sbuf_tensor("wpk", [128, 2 * 1024], BF16))
            wpv = ph_a.enter_context(nc.sbuf_tensor("wpv", [128, 2 * 1024], BF16))
            wcf = ph_a.enter_context(nc.sbuf_tensor("wcf", [128, 2 * 1024], F32))
            wcb = ph_a.enter_context(nc.sbuf_tensor("wcb", [128, 2 * 1024], BF16))
            xt = ph_a.enter_context(nc.sbuf_tensor("xt", [128, 2 * 1024], F32))
            qt = ph_a.enter_context(nc.sbuf_tensor("qt", [128, 2 * T], BF16))
            kt = ph_a.enter_context(nc.sbuf_tensor("kt", [128, 2 * T], BF16))
            vt = ph_a.enter_context(nc.sbuf_tensor("vt", [128, 2 * NT * 129],
                                                   BF16))
            es = ph_a.enter_context(nc.sbuf_tensor("es", [128, 2 * 2048], BF16))
            nd = ph_a.enter_context(nc.sbuf_tensor("nd", [128, 2048], F32))
            astg = ph_a.enter_context(nc.sbuf_tensor("astg", [128, 2 * 1024], F32))
            psP = psS

            P.emit("sync", lambda En: En.dma_start(out=idf[:], in_=idf_d)
                   .then_inc(S("sMISC").h, 16))
            S("sMISC").n += 16
            P.emit("sync", lambda En: En.dma_start(
                out=astg[:, 0:64].rearrange("p (i e) -> p i e", e=8),
                in_=rt_d.rearrange("(i p) e -> p i e", p=128))
                .then_inc(S("sMISC").h, 16))
            S("sMISC").n += 16
            P.wait("vector", S("sMISC"), 32)
            P.emit("vector", lambda En: En.tensor_copy(rt_bf[:], astg[:, 0:64])
                   .then_inc(S("sMISC").h, 1))
            S("sMISC").n += 1
            P.emit("vector", lambda En: En.memset(
                vt[:].rearrange("p (j c) -> p j c", c=129)[:, :, 64:65], 1.0)
                .then_inc(S("sONE").h, 1))
            S("sONE").n += 1

            def wpair_load_cast(p):
                b = p % 2
                for wi, wsrc in enumerate((wq_d, wk_d, wv_d)):
                    prev = 3 * p + wi - 1
                    P.wait("gpsimd", S("sWC"),
                           S("sWC").at(prev) if prev >= 0 else 0)
                    P.emit("gpsimd", lambda En, i=wsrc, p=p: En.dma_start(
                        out=wst[:].rearrange("p (i m) -> p i m", m=128),
                        in_=i[:, p * 128:(p + 1) * 128]
                            .rearrange("(i p) m -> p i m", p=128))
                        .then_inc(S("sWP").h, 16))
                    S("sWP").n += 16
                    S("sWP").mark(3 * p + wi)
                    dstt = (wpq, wpk, wpv)[wi]
                    P.wait("vector", S("sWP"), S("sWP").at(3 * p + wi))
                    if p >= 2 and wi == 0:
                        P.wait("vector", S("sQKVm"),
                               S("sQKVm").at(("pairdone", p - 1)))
                    P.emit("vector", lambda En, o=dstt, b=b: En.tensor_copy(
                        o[:, b * 1024:(b + 1) * 1024], wst[:])
                        .then_inc(S("sWC").h, 1))
                    S("sWC").n += 1
                    S("sWC").mark(3 * p + wi)
                S("sWC").mark(("pair", p))

            wpair_load_cast(0)

            def ln_stage1(j):
                b = j % 2
                s0 = b * 8
                xtb = xt[:, b * 1024:(b + 1) * 1024]
                P.wait("sync", S("sTR"), S("sTR").at(j - 2) if j >= 2 else 0)
                P.emit("sync", lambda En, j=j, o=xtb: En.dma_start(
                    out=o, in_=x_d[j * 128:(j + 1) * 128, :])
                    .then_inc(Sp("sX", j).h, 16))
                Sp("sX", j).n += 16
                Sp("sX", j).mark(j)
                P.wait("vector", Sp("sX", j), Sp("sX", j).at(j))
                P.emit("vector", lambda En, i=xtb, s0=s0: En.reduce_sum(
                    stat[:, s0:s0 + 1], i, axis=AX.X).then_inc(S("sLNa").h, 1))
                S("sLNa").n += 1
                P.wait("vector", S("sLNa"), S("sLNa").n)
                P.emit("vector", lambda En, s0=s0: En.tensor_scalar(
                    stat[:, s0 + 1:s0 + 2], stat[:, s0:s0 + 1], 1.0 / E, None,
                    ALU.mult).then_inc(S("sLNa").h, 1))
                S("sLNa").n += 1
                P.wait("vector", S("sLNa"), S("sLNa").n)
                P.emit("vector", lambda En, i=xtb, s0=s0: En.tensor_scalar(
                    i, i, stat[:, s0 + 1:s0 + 2], None, ALU.subtract)
                    .then_inc(S("sLNa").h, 1))
                S("sLNa").n += 1
                S("sLNa").mark(("xc", j))
                P.wait("scalar", S("sLNa"), S("sLNa").at(("xc", j)))
                if j >= 2:
                    P.wait("scalar", S("sLNa"), S("sLNa").at(("ve", j - 2)))
                if j == 0:
                    P.wait("scalar", S("sMISC"), 33)
                P.emit("scalar", lambda En, i=xtb, b=b, s0=s0: En.activation(
                    astg[:, b * 1024:(b + 1) * 1024], i, AF.Square,
                    accum_out=stat[:, s0 + 2:s0 + 3]).then_inc(S("sLNb").h, 1))
                S("sLNb").n += 1
                S("sLNb").mark(("sq", j))

            def ln_stage2(j):
                b = j % 2
                s0 = b * 8
                xtb = xt[:, b * 1024:(b + 1) * 1024]
                P.wait("vector", S("sLNb"), S("sLNb").at(("sq", j)))
                P.emit("vector", lambda En, s0=s0: En.tensor_scalar(
                    stat[:, s0 + 3:s0 + 4], stat[:, s0 + 2:s0 + 3], 1.0 / E,
                    EPS, ALU.mult, ALU.add).then_inc(S("sLNa").h, 1))
                S("sLNa").n += 1
                S("sLNa").mark(("ve", j))
                P.wait("scalar", S("sLNa"), S("sLNa").at(("ve", j)))
                if j >= 2:
                    P.wait("scalar", S("sLNc"), S("sLNc").at(j - 2))
                P.emit("scalar", lambda En, s0=s0: En.activation(
                    stat[:, s0 + 4:s0 + 5], stat[:, s0 + 3:s0 + 4], AF.Ln)
                    .then_inc(S("sLNb").h, 1))
                S("sLNb").n += 1
                P.wait("scalar", S("sLNb"), S("sLNb").n)
                P.emit("scalar", lambda En, s0=s0: En.activation(
                    stat[:, s0 + 5:s0 + 6], stat[:, s0 + 4:s0 + 5], AF.Exp,
                    scale=-0.5).then_inc(S("sLNb").h, 1))
                S("sLNb").n += 1
                S("sLNb").mark(("rs", j))
                P.wait("vector", S("sLNb"), S("sLNb").at(("rs", j)))
                P.emit("vector", lambda En, i=xtb, s0=s0: En.tensor_scalar(
                    i, i, stat[:, s0 + 5:s0 + 6], None, ALU.mult)
                    .then_inc(S("sLNc").h, 1))
                S("sLNc").n += 1
                S("sLNc").mark(j)
                P.wait("tensor", S("sLNc"), S("sLNc").at(j))
                if j >= 2:
                    P.wait("tensor", S("sXNT"), S("sXNT").at(j - 2))
                if j == 0:
                    P.wait("tensor", S("sMISC"), 32)
                for c in range(8):
                    TRN(psP[b][:, c * 128:(c + 1) * 128],
                        xtb[:, c * 128:(c + 1) * 128],
                        128, sem=S("sTR") if c == 7 else None)
                S("sTR").mark(j)
                P.wait("vector", S("sTR"), S("sTR").at(j))
                P.emit("vector", lambda En, j=j, b=b: En.tensor_copy(
                    xnt[:].rearrange("p (i t) -> p i t", t=T)
                        [:, :, j * 128:(j + 1) * 128],
                    psP[b][:].rearrange("p (i t) -> p i t", t=128))
                    .then_inc(S("sXNT").h, 1))
                S("sXNT").n += 1
                S("sXNT").mark(j)

            for j in range(NT + 1):
                if j < NT:
                    ln_stage1(j)
                if j >= 1:
                    ln_stage2(j - 1)

            if stage == "ln":
                P.wait("sync", S("sXNT"), S("sXNT").at(NT - 1))
                P.emit("sync", lambda En: En.dma_start(out=dbg_d, in_=xnt[:])
                       .then_inc(S("sMISC").h, 16))
                S("sMISC").n += 16
                P.wait("sync", S("sMISC"), S("sMISC").n)
                P.flush(block)
                return nc

            pieces = []
            for e in range(NEXP):
                g, u, d = psrc(e)
                for r in range(4):
                    pieces.append((g[256 * r:256 * (r + 1), :],
                                   gbf_d[e][:, r * 1024:(r + 1) * 1024]
                                   .rearrange("p (i c) -> p i c", c=512)))
                for r in range(4):
                    pieces.append((u[256 * r:256 * (r + 1), :],
                                   ubf_d[e][:, r * 1024:(r + 1) * 1024]
                                   .rearrange("p (i c) -> p i c", c=512)))
                for ch in range(2):
                    for r in range(2):
                        pieces.append((d[256 * r:256 * (r + 1),
                                         512 * ch:512 * (ch + 1)],
                                       dbf_d[e].rearrange("p (k h) -> p k h", h=1024)
                                       [:, 2 * r:2 * r + 2, ch * 512:(ch + 1) * 512]))
            NPC = len(pieces)

            def precast_step(k):
                b = k % 2
                src, dst = pieces[k]
                fbuf = wcf[:, b * 1024:(b + 1) * 1024]
                bbuf = wcb[:, b * 1024:(b + 1) * 1024]
                P.wait("gpsimd", S("sWCc"), S("sWCc").at(k - 2) if k >= 2 else 0)
                P.emit("gpsimd", lambda En, o=fbuf, i=src: En.dma_start(
                    out=o.rearrange("p (i c) -> p i c", c=512),
                    in_=i.rearrange("(i p) c -> p i c", p=128))
                    .then_inc(Sp("sWCf", k).h, 16))
                Sp("sWCf", k).n += 16
                Sp("sWCf", k).mark(k)
                P.wait("vector", Sp("sWCf", k), Sp("sWCf", k).at(k))
                if k >= 2:
                    P.wait("vector", Sp("sWCo", k), Sp("sWCo", k).at(k - 2))
                P.emit("vector", lambda En, o=bbuf, i=fbuf: En.tensor_copy(o, i)
                       .then_inc(S("sWCc").h, 1))
                S("sWCc").n += 1
                S("sWCc").mark(k)
                P.wait("gpsimd", S("sWCc"), S("sWCc").at(k))
                P.emit("gpsimd", lambda En, o=dst, i=bbuf, kk=k: En.dma_start(
                    out=o, in_=i.rearrange("p (i c) -> p i c", c=512))
                    .then_inc(Sp("sWCo", kk).h, 16))
                Sp("sWCo", k).n += 16
                Sp("sWCo", k).mark(k)

            precast_iter = iter(range(NPC))

            NC2 = T // 512

            def att_prev(p, c, j, k):
                g = (p * NC2 + c) * KT + j - k
                if g < 0:
                    return None
                pp, r = divmod(g, NC2 * KT)
                cc, jj = divmod(r, KT)
                return (pp, cc, jj)

            def S_emit(p, c, j):
                q0 = (p % 2) * T + c * 512
                prev = att_prev(p, c, j, 2)
                if prev is not None:
                    P.wait("tensor", S("sEXP"), S("sEXP").at(prev))
                if c == 0 and j == 0:
                    P.wait("tensor", S("sQKVe"), S("sQKVe").at(("allqkv", p)))
                for hi, ktb in ((0, 0), (1, 64)):
                    MM(psS[j % 2][:, hi * 512:(hi + 1) * 512],
                       kt[ktb:ktb + 64, (p % 2) * T + j * 128:
                          (p % 2) * T + (j + 1) * 128],
                       qt[ktb:ktb + 64, q0:q0 + 512],
                       start=True, stop=True,
                       sem=S("sS") if hi == 1 else None)
                S("sS").mark((p, c, j))

            def EXP_emit(p, c, j):
                P.wait("scalar", S("sS"), S("sS").at((p, c, j)))
                prev = att_prev(p, c, j, 4)
                if prev is not None:
                    P.wait("scalar", S("sPVA"), S("sPVA").at(prev))
                P.emit("scalar", lambda En, j4=j % 4, j2=j % 2: En.activation(
                    es[:, j4 * 1024:(j4 + 1) * 1024], psS[j2][:, 0:1024],
                    AF.Exp, scale=0.125).then_inc(S("sEXP").h, 1))
                S("sEXP").n += 1
                S("sEXP").mark((p, c, j))

            def PV_emit(p, c, jj):
                P.wait("tensor", S("sEXP"), S("sEXP").at((p, c, jj)))
                if jj == 0:
                    P.wait("tensor", S("sONE"), 1)
                    P.wait("tensor", S("sVTe"), S("sVTe").at((p, NT - 1)))
                    if c >= 1:
                        P.wait("tensor", S("sNDA"), S("sNDA").at((p, c - 1)))
                    elif p >= 1:
                        P.wait("tensor", S("sNDA"),
                               S("sNDA").at((p - 1, NC2 - 1)))
                vo = (p % 2) * (NT * 129)
                for hi, voff in ((0, 0), (1, 64)):
                    MM(ps2[0:65, hi * 512:(hi + 1) * 512],
                       vt[:, vo + jj * 129 + voff:vo + jj * 129 + voff + 65],
                       es[:, (jj % 4) * 1024 + hi * 512:
                          (jj % 4) * 1024 + (hi + 1) * 512],
                       start=(jj == 0), stop=(jj == KT - 1),
                       sem=S("sPVA") if hi == 1 else None)
                S("sPVA").mark((p, c, jj))

            ps3c = {"A": None, "B": None}

            def chain_wait(half):
                tok = ps3c[half]
                if tok is not None:
                    P.wait("tensor", tok[0], tok[1])

            def qkv_gen(p):
                b = p % 2
                for (wbuf, dst, pi) in ((wpq, qt, 0), (wpk, kt, 1)):
                    for n in range(NQC):
                        nh = n % 2
                        half = "AB"[nh]
                        chain_wait(half)
                        if pi == 0 and n == 0:
                            P.wait("tensor", S("sWC"), S("sWC").at(("pair", p)))
                        if p == 0:
                            P.wait("tensor", S("sXNT"), S("sXNT").at(
                                min(NT - 1, (n + 1) * 4 - 1)))
                        for c in range(NE):
                            MM(ps3[:, nh * 512:(nh + 1) * 512],
                               wbuf[:, b * 1024 + c * 128:
                                    b * 1024 + (c + 1) * 128],
                               xnt[:, c * T + n * 512:c * T + (n + 1) * 512],
                               start=(c == 0), stop=(c == NE - 1),
                               sem=S("sQKVm") if c == NE - 1 else None)
                        S("sQKVm").mark((p, pi, n))
                        P.wait("vector", S("sQKVm"), S("sQKVm").at((p, pi, n)))
                        if p >= 2 and pi == 0 and n == 0:
                            P.wait("vector", S("sS"), S("sS").at(("pair", p - 2)))
                        P.emit("vector", lambda En, o=dst, n=n, nh=nh, b=b:
                               En.tensor_copy(
                            o[:, b * T + n * 512:b * T + (n + 1) * 512],
                            ps3[:, nh * 512:(nh + 1) * 512])
                            .then_inc(S("sQKVe").h, 1))
                        S("sQKVe").n += 1
                        S("sQKVe").mark((p, pi, n))
                        ps3c[half] = (S("sQKVe"), S("sQKVe").n)
                        yield
                for n in range(NQC):
                    nh = n % 2
                    half = "AB"[nh]
                    chain_wait(half)
                    for c in range(NE):
                        MM(ps3[:, nh * 512:(nh + 1) * 512],
                           wpv[:, b * 1024 + c * 128:b * 1024 + (c + 1) * 128],
                           xnt[:, c * T + n * 512:c * T + (n + 1) * 512],
                           start=(c == 0), stop=(c == NE - 1),
                           sem=S("sQKVm") if c == NE - 1 else None)
                    S("sQKVm").mark((p, "v", n))
                    P.wait("vector", S("sQKVm"), S("sQKVm").at((p, "v", n)))
                    vtf = xt[:, nh * 1024:nh * 1024 + 512]
                    if n >= 2:
                        P.wait("vector", S("sVT"), S("sVT").at((p, (n - 2) * 4 + 3)))
                    elif p >= 1:
                        P.wait("vector", S("sVT"), S("sVT").at(("pair", p - 1)))
                    else:
                        P.wait("vector", S("sLNc"), S("sLNc").n)
                    P.emit("vector", lambda En, o=vtf, nh=nh: En.tensor_copy(
                        o, ps3[:, nh * 512:(nh + 1) * 512])
                        .then_inc(S("sQKVe").h, 1))
                    S("sQKVe").n += 1
                    S("sQKVe").mark((p, "v", n))
                    ps3c[half] = (S("sQKVe"), S("sQKVe").n)
                    yield
                    chain_wait(half)
                    P.wait("tensor", S("sQKVe"), S("sQKVe").at((p, "v", n)))
                    for k in range(4):
                        TRN(ps3[:, nh * 512 + k * 128:nh * 512 + (k + 1) * 128],
                            vtf[:, k * 128:(k + 1) * 128],
                            128, sem=S("sVT") if k == 3 else None)
                        S("sVT").mark((p, n * 4 + k))
                    P.wait("vector", S("sVT"), S("sVT").at((p, n * 4 + 3)))
                    if p >= 2 and n == 0:
                        P.wait("vector", S("sPVA"), S("sPVA").at(("pair", p - 2)))
                    for k in range(4):
                        tt = n * 4 + k
                        col = nh * 512 + k * 128
                        vb = b * (NT * 129)
                        P.emit("vector", lambda En, tt=tt, col=col, vb=vb:
                               En.tensor_copy(vt[:, vb + tt * 129:vb + tt * 129 + 64],
                                              ps3[:, col:col + 64]))
                        P.emit("vector", lambda En, tt=tt, col=col, vb=vb:
                               En.tensor_copy(
                            vt[:, vb + tt * 129 + 65:vb + tt * 129 + 129],
                            ps3[:, col + 64:col + 128])
                            .then_inc(S("sVTe").h, 1))
                        S("sVTe").n += 1
                        S("sVTe").mark((p, tt))
                    ps3c[half] = (S("sVTe"), S("sVTe").n)
                    yield
                S("sVT").mark(("pair", p))
                S("sQKVm").mark(("pairdone", p))
                S("sQKVe").mark(("allqkv", p))

            pend = None
            for p in range(8):
                b = p % 2
                if p + 1 < 8:
                    wpair_load_cast(p + 1)
                if p == 0:
                    for _ in qkv_gen(0):
                        pass
                pend = qkv_gen(p + 1) if p + 1 < 8 else None

                if stage == "qkv" and p == 0:
                    P.wait("sync", S("sQKVe"), S("sQKVe").at(("allqkv", 0)))
                    P.wait("sync", S("sVTe"), S("sVTe").at((0, NT - 1)))
                    P.emit("sync", lambda En: En.dma_start(
                        out=dbg_d[:, 0:T], in_=qt[:]).then_inc(S("sMISC").h, 16))
                    S("sMISC").n += 16
                    P.emit("sync", lambda En: En.dma_start(
                        out=dbg_d[:, T:2 * T], in_=kt[:]).then_inc(S("sMISC").h, 16))
                    S("sMISC").n += 16
                    P.emit("sync", lambda En: En.dma_start(
                        out=dbg_d[:, 2 * T:2 * T + NT * 129], in_=vt[:])
                        .then_inc(S("sMISC").h, 16))
                    S("sMISC").n += 16
                    P.wait("sync", S("sMISC"), S("sMISC").n)
                    P.flush(block)
                    return nc

                for c in range(NC2):
                    pre = 2 if c > 0 else 0
                    for j in range(KT):
                        if j >= pre:
                            S_emit(p, c, j)
                        EXP_emit(p, c, j)
                        if j >= 2:
                            PV_emit(p, c, j - 2)
                        if j % 8 == 7 and pend is not None:
                            next(pend, None)
                    PV_emit(p, c, KT - 2)
                    PV_emit(p, c, KT - 1)
                    if c + 1 < NC2:
                        S_emit(p, c + 1, 0)
                        S_emit(p, c + 1, 1)
                    gidx = p * NC2 + c
                    cs = gidx % 4
                    stg = astg[:, cs * 512:(cs + 1) * 512]
                    P.wait("vector", S("sPVA"), S("sPVA").at((p, c, KT - 1)))
                    P.emit("vector", lambda En: En.tensor_copy(
                        nd[0:65, 0:1024], ps2[0:65, 0:1024])
                        .then_inc(S("sNDA").h, 1))
                    S("sNDA").n += 1
                    S("sNDA").mark((p, c))
                    P.wait("tensor", S("sNDA"), S("sNDA").at((p, c)))
                    chain_wait("A")
                    chain_wait("B")
                    for hi, ndo in ((0, 0), (1, 512)):
                        for t in range(4):
                            TRN(ps3[:, ndo + t * 128:ndo + t * 128 + 65],
                                nd[0:65, ndo + t * 128:ndo + (t + 1) * 128],
                                65, sem=S("sTRA") if (hi == 1 and t == 3)
                                else None)
                    S("sTRA").mark((p, c))
                    P.wait("vector", S("sTRA"), S("sTRA").at((p, c)))
                    P.emit("vector", lambda En: En.tensor_copy(
                        nd[:, 0:1024], ps3[:, 0:1024])
                        .then_inc(S("sDIV").h, 1))
                    S("sDIV").n += 1
                    S("sDIV").mark((p, c))
                    ps3c["A"] = (S("sDIV"), S("sDIV").n)
                    ps3c["B"] = (S("sDIV"), S("sDIV").n)
                    for (hh, ndo) in (("A", 0), ("B", 512)):
                        dcol = 64 if hh == "A" else 0
                        ncol0 = 0 if hh == "A" else 1
                        scol = 0 if hh == "A" else 64
                        if hh == "A":
                            P.wait("vector", S("sDIV"), S("sDIV").at((p, c)))
                        if hh == "A" and gidx >= 4:
                            P.wait("vector", SST(gidx), SST(gidx).at(gidx - 4))
                        for t in range(4):
                            if not (hh == "A" and t == 0):
                                P.wait("vector", S("sDV2"), S("sDV2").n)
                            P.emit("vector", lambda En, ndo=ndo, t=t, dcol=dcol:
                                   En.reciprocal(
                                stat[:, 16:17],
                                nd[:, ndo + t * 128 + dcol:
                                   ndo + t * 128 + dcol + 1])
                                .then_inc(S("sSMv").h, 1))
                            S("sSMv").n += 1
                            P.wait("vector", S("sSMv"), S("sSMv").n)
                            P.emit("vector", lambda En, ndo=ndo, t=t, n0=ncol0,
                                   sc=scol, stg=stg: En.tensor_scalar(
                                stg[:, t * 128 + sc:t * 128 + sc + 64],
                                nd[:, ndo + t * 128 + n0:ndo + t * 128 + n0 + 64],
                                stat[:, 16:17], None, ALU.mult)
                                .then_inc(S("sDV2").h, 1))
                            S("sDV2").n += 1
                    S("sDV2").mark((p, c))
                    P.wait("sync", S("sDV2"), S("sDV2").at((p, c)))
                    P.emit("sync", lambda En, p=p, c=c, stg=stg,
                           sh=SST(gidx).h: En.dma_start(
                        out=attn_d.rearrange("(J p) e -> p J e", p=128)
                            [:, c * 4:(c + 1) * 4, p * 128:(p + 1) * 128],
                        in_=stg.rearrange("p (J e) -> p J e", e=128))
                        .then_inc(sh, 16))
                    SST(gidx).n += 16
                    SST(gidx).mark(gidx)
                    for _ in range(2):
                        k = next(precast_iter, None)
                        if k is not None:
                            precast_step(k)
                if pend is not None:
                    for _ in pend:
                        pass
                S("sEXP").mark(("pair", p))
                S("sS").mark(("pair", p))
                S("sPVA").mark(("pair", p))
            for k in precast_iter:
                precast_step(k)
            for i2 in range(4):
                SST(i2).mark("all")
            for i2 in range(2):
                Sp("sWCo", i2).mark("all")
            S("sDIV").mark("attn_done")
            S("sEXP").mark("attn_done")

            if stage == "attn":
                for i2 in range(4):
                    P.wait("sync", SST(i2), SST(i2).at("all"))
                for i2 in range(2):
                    P.wait("sync", Sp("sWCo", i2), Sp("sWCo", i2).at("all"))
                P.flush(block)
                return nc

            P.flush(block)

        with contextlib.ExitStack() as ph_f:
            psD = ph_f.enter_context(nc.psum_tensor("psD", [128, 2048], F32))
            acc = ph_f.enter_context(nc.sbuf_tensor("acc", [128, 8 * 1024], F32))
            wg = ph_f.enter_context(nc.sbuf_tensor("wg", [128, 2 * 4096], BF16))
            wu = ph_f.enter_context(nc.sbuf_tensor("wu", [128, 2 * 4096], BF16))
            wd = ph_f.enter_context(nc.sbuf_tensor("wd", [128, 2 * 4096], BF16))
            m_sb = ph_f.enter_context(nc.sbuf_tensor("m_sb", [128, 2 * 4096], BF16))
            xres = ph_f.enter_context(nc.sbuf_tensor("xres", [128, 2 * 1024], F32))
            ares = ph_f.enter_context(nc.sbuf_tensor("ares", [128, 2 * 1024], F32))
            sl = ph_f.enter_context(nc.sbuf_tensor("sl", [128, 1024], F32))

            def phase_gate(eng):
                for i2 in range(2):
                    P.wait(eng, Sp("sWCo", i2), Sp("sWCo", i2).at("all"))
                for i2 in range(4):
                    P.wait(eng, SST(i2), SST(i2).at("all"))
                P.wait(eng, S("sDIV"), S("sDIV").at("attn_done"))
                P.wait(eng, S("sEXP"), S("sEXP").at("attn_done"))
                P.wait(eng, S("sWC"), S("sWC").n)
                P.wait(eng, S("sVTe"), S("sVTe").n)
                P.wait(eng, S("sQKVe"), S("sQKVe").n)
            for eng in ("tensor", "vector", "scalar", "sync", "gpsimd"):
                phase_gate(eng)
            for g in range(NT // 4):
                cb = (g % 2) * 1024
                if g >= 2:
                    P.wait("tensor", S("sRTe"), S("sRTe").at(g - 2))
                for t4 in range(4):
                    tt = g * 4 + t4
                    for i in range(NE):
                        MM(psD[:, cb + t4 * 128:cb + t4 * 128 + 8],
                           xnt[:, i * T + tt * 128:i * T + (tt + 1) * 128],
                           rt_bf[:, i * 8:(i + 1) * 8],
                           start=(i == 0), stop=(i == NE - 1),
                           sem=S("sRT") if (t4 == 3 and i == NE - 1) else None)
                S("sRT").mark(g)
                P.wait("vector", S("sRT"), S("sRT").at(g))
                P.emit("vector", lambda En, cb=cb, g=g: En.tensor_copy(
                    logit[:, g * 32:(g + 1) * 32].rearrange("p (t r) -> p t r", r=8),
                    psD[:].rearrange("p (t r) -> p t r", r=128)
                        [:, 8 * (cb // 1024):8 * (cb // 1024) + 4, 0:8])
                    .then_inc(S("sRTe").h, 1))
                S("sRTe").n += 1
                S("sRTe").mark(g)
            S("sRTe").mark("all")
            for tt in range(NT):
                L = logit[:, tt * 8:(tt + 1) * 8]
                P.wait("vector", S("sRTe"), S("sRTe").at(tt // 4))
                c0, c1, c2, c3, c4 = (stat[:, 9:10], stat[:, 10:11], stat[:, 11:12],
                                      stat[:, 12:13], stat[:, 13:14])
                pbuf, ge1, msk, ge2 = (st8[:, 0:8], st8[:, 8:16],
                                       st8[:, 16:24], st8[:, 24:32])
                P.emit("vector", lambda En, L=L, o=c0: En.reduce_max(
                    o, L, axis=AX.X, negate=True).then_inc(S("sSMv").h, 1))
                S("sSMv").n += 1
                P.emit("vector", lambda En, L=L, o=c1: En.reduce_max(
                    o, L, axis=AX.X).then_inc(S("sSMv").h, 1))
                S("sSMv").n += 1
                P.wait("scalar", S("sSMv"), S("sSMv").n)
                P.emit("scalar", lambda En, L=L, o=pbuf, bb=c0, a=c2: En.activation(
                    o, L, AF.Exp, bias=bb, accum_out=a).then_inc(S("sSMs").h, 1))
                S("sSMs").n += 1
                S("sSMs").mark(("exp", tt))
                P.wait("vector", S("sSMs"), S("sSMs").at(("exp", tt)))
                P.emit("vector", lambda En, L=L, o=ge1, m=c1: En.tensor_single_scalar(
                    o, L, m, ALU.is_ge).then_inc(S("sSMv").h, 1))
                S("sSMv").n += 1
                P.wait("vector", S("sSMv"), S("sSMv").n)
                P.emit("vector", lambda En, L=L, o=msk, g1=ge1: En.scalar_tensor_tensor(
                    o, g1, -1e30, L, ALU.mult, ALU.add).then_inc(S("sSMv").h, 1))
                S("sSMv").n += 1
                P.wait("vector", S("sSMv"), S("sSMv").n)
                P.emit("vector", lambda En, o=c3, m=msk: En.reduce_max(
                    o, m, axis=AX.X).then_inc(S("sSMv").h, 1))
                S("sSMv").n += 1
                P.wait("vector", S("sSMv"), S("sSMv").n)
                P.emit("vector", lambda En, L=L, o=ge2, m=c3: En.tensor_single_scalar(
                    o, L, m, ALU.is_ge).then_inc(S("sSMv").h, 1))
                S("sSMv").n += 1
                P.emit("vector", lambda En, o=c4, i=c2: En.reciprocal(o, i)
                       .then_inc(S("sSMv").h, 1))
                S("sSMv").n += 1
                P.wait("vector", S("sSMv"), S("sSMv").n)
                P.emit("vector", lambda En, o=pbuf, r=c4: En.tensor_scalar(
                    o, o, r, None, ALU.mult).then_inc(S("sSMv").h, 1))
                S("sSMv").n += 1
                P.wait("vector", S("sSMv"), S("sSMv").n)
                P.emit("vector", lambda En, o=coef[:, tt * 8:(tt + 1) * 8],
                       a=pbuf, g2=ge2: En.tensor_mul(o, a, g2)
                       .then_inc(S("sSMv").h, 1))
                S("sSMv").n += 1
            S("sSMv").mark("coef_all")

            def ffn_wload(idx, e):
                eb = idx % 2
                for eng in ("sync", "scalar"):
                    P.wait(eng, Sp("sWCo", 0), Sp("sWCo", 0).at(12 * e + 10))
                    P.wait(eng, Sp("sWCo", 1), Sp("sWCo", 1).at(12 * e + 11))
                    if idx >= 2:
                        P.wait(eng, S("sTEN"), S("sTEN").at(("edone", idx - 2)))
                for (eng, wdst, wsrc) in (("sync", wg, gbf_d),
                                          ("scalar", wu, ubf_d),
                                          ("sync", wd, dbf_d)):
                    P.emit(eng, lambda En, e=e, eb=eb, o=wdst, i=wsrc,
                           sh=Sp("sFW", idx).h: En.dma_start(
                        out=o[:, eb * 4096:(eb + 1) * 4096], in_=i[e])
                        .then_inc(sh, 16))
                    Sp("sFW", idx).n += 16
                Sp("sFW", idx).mark(idx)

            ffn_wload(0, 0)
            ffn_wload(1, 1)
            for cch in range(NCH):
                for t in range(8):
                    tb = t % 2
                    gt = cch * 8 + t
                    if t >= 2:
                        P.wait("sync", S("sINIT"), S("sINIT").at((cch, t - 2)))
                    elif cch >= 1:
                        P.wait("sync", S("sINIT"), S("sINIT").at((cch - 1, 6 + t)))
                    for i2 in range(4):
                        P.wait("sync", SST(i2), SST(i2).at("all"))
                    P.emit("sync", lambda En, gt=gt, tb=tb, sh=Sp("sXR", t).h:
                           En.dma_start(
                        out=xres[:, tb * 1024:(tb + 1) * 1024],
                        in_=x_d[gt * 128:(gt + 1) * 128, :]).then_inc(sh, 16))
                    Sp("sXR", t).n += 16
                    P.emit("sync", lambda En, gt=gt, tb=tb, sh=Sp("sXR", t).h:
                           En.dma_start(
                        out=ares[:, tb * 1024:(tb + 1) * 1024],
                        in_=attn_d[gt * 128:(gt + 1) * 128, :]).then_inc(sh, 16))
                    Sp("sXR", t).n += 16
                    Sp("sXR", t).mark((cch, t))
                    P.wait("vector", Sp("sXR", t), Sp("sXR", t).at((cch, t)))
                    if cch >= 1:
                        P.wait("vector", S("sOUT"), S("sOUT").at(("chunk", cch - 1)))
                    P.emit("vector", lambda En, t=t, tb=tb: En.tensor_add(
                        acc[:, t * 1024:(t + 1) * 1024],
                        xres[:, tb * 1024:(tb + 1) * 1024],
                        ares[:, tb * 1024:(tb + 1) * 1024]).then_inc(S("sINIT").h, 1))
                    S("sINIT").n += 1
                    S("sINIT").mark((cch, t))

                for e in range(NEXP):
                    idx = cch * NEXP + e
                    eb = idx % 2
                    me = idx % 2
                    for k in range(4):
                        for (wsrc, psb, nm) in ((wg, ps2, "g"), (wu, ps3, "u")):
                            if nm == "g":
                                if k >= 1:
                                    P.wait("tensor", S("sSL"), S("sSL").at((idx, k - 1)))
                                elif idx >= 1:
                                    P.wait("tensor", S("sSL"), S("sSL").at((idx - 1, 3)))
                                elif stage == "full":
                                    pass
                            else:
                                if k >= 1:
                                    P.wait("tensor", S("sMUL"),
                                           S("sMUL").at((idx, k - 1)))
                                elif idx >= 1:
                                    P.wait("tensor", S("sMUL"),
                                           S("sMUL").at((idx - 1, 3)))
                            if k == 0 and nm == "g":
                                P.wait("tensor", Sp("sFW", idx), Sp("sFW", idx).at(idx))
                            for i in range(NE):
                                for hf in range(2):
                                    MM(psb[:, hf * 512:(hf + 1) * 512],
                                       wsrc[:, eb * 4096 + i * 512 + k * 128:
                                            eb * 4096 + i * 512 + (k + 1) * 128],
                                       xnt[:, i * T + cch * 1024 + hf * 512:
                                           i * T + cch * 1024 + (hf + 1) * 512],
                                       start=(i == 0), stop=(i == NE - 1),
                                       sem=S("sTEN") if (i == NE - 1 and hf == 1)
                                       else None)
                            S("sTEN").mark((idx, nm, k))
                        P.wait("scalar", S("sTEN"), S("sTEN").at((idx, "g", k)))
                        if k >= 1:
                            P.wait("scalar", S("sMUL"), S("sMUL").at((idx, k - 1)))
                        elif idx >= 1:
                            P.wait("scalar", S("sMUL"), S("sMUL").at((idx - 1, 3)))
                        P.emit("scalar", lambda En: En.activation(
                            sl[:], ps2[:, 0:1024], AF.Silu).then_inc(S("sSL").h, 1))
                        S("sSL").n += 1
                        S("sSL").mark((idx, k))
                        P.wait("vector", S("sSL"), S("sSL").at((idx, k)))
                        P.wait("vector", S("sTEN"), S("sTEN").at((idx, "u", k)))
                        if idx >= 2 and k == 0:
                            P.wait("vector", S("sTEN"), S("sTEN").at(("edone", idx - 2)))
                        P.emit("vector", lambda En, me=me, k=k: En.tensor_mul(
                            m_sb[:, me * 4096 + k * 1024:me * 4096 + (k + 1) * 1024],
                            sl[:], ps3[:, 0:1024]).then_inc(S("sMUL").h, 1))
                        S("sMUL").n += 1
                        S("sMUL").mark((idx, k))
                    for t in range(8):
                        pcol = (t % 2) * 1024
                        if t >= 2:
                            P.wait("tensor", S("sACC"), S("sACC").at((idx, t - 2)))
                        elif idx >= 1:
                            P.wait("tensor", S("sACC"), S("sACC").at((idx - 1, 6 + t)))
                        else:
                            P.wait("tensor", S("sRTe"), S("sRTe").at("all"))
                        if t == 0:
                            P.wait("tensor", S("sMUL"), S("sMUL").at((idx, 3)))
                        for k in range(4):
                            for hf in range(2):
                                MM(psD[:, pcol + hf * 512:pcol + (hf + 1) * 512],
                                   m_sb[:, me * 4096 + k * 1024 + t * 128:
                                        me * 4096 + k * 1024 + (t + 1) * 128],
                                   wd[:, eb * 4096 + k * 1024 + hf * 512:
                                      eb * 4096 + k * 1024 + (hf + 1) * 512],
                                   start=(k == 0), stop=(k == 3),
                                   sem=S("sTEN") if (hf == 1 and k == 3) else None)
                        S("sTEN").mark((idx, "d", t))
                        P.wait("vector", S("sTEN"), S("sTEN").at((idx, "d", t)))
                        P.wait("vector", S("sINIT"), S("sINIT").at((cch, t)))
                        if e >= 2:
                            P.wait("vector", S("sSMv"), S("sSMv").at("coef_all"))
                        if e >= 1:
                            P.wait("vector", S("sACC"), S("sACC").at((idx - 1, t)))
                        gt = cch * 8 + t
                        if e < 2:
                            P.emit("vector", lambda En, pcol=pcol, t=t: En.tensor_add(
                                acc[:, t * 1024:(t + 1) * 1024],
                                acc[:, t * 1024:(t + 1) * 1024],
                                psD[:, pcol:pcol + 1024]).then_inc(S("sACC").h, 1))
                        else:
                            P.emit("vector", lambda En, pcol=pcol, t=t, gt=gt, e=e:
                                   En.scalar_tensor_tensor(
                                acc[:, t * 1024:(t + 1) * 1024],
                                psD[:, pcol:pcol + 1024],
                                coef[:, gt * 8 + (e - 2):gt * 8 + (e - 2) + 1],
                                acc[:, t * 1024:(t + 1) * 1024],
                                ALU.mult, ALU.add).then_inc(S("sACC").h, 1))
                        S("sACC").n += 1
                        S("sACC").mark((idx, t))
                    S("sTEN").mark(("edone", idx))
                    if idx + 2 < NCH * NEXP:
                        ffn_wload(idx + 2, (idx + 2) % NEXP)
                for t in range(8):
                    gt = cch * 8 + t
                    P.wait("sync", S("sACC"), S("sACC").at((cch * NEXP + NEXP - 1, t)))
                    P.emit("sync", lambda En, gt=gt, t=t: En.dma_start(
                        out=out_d[gt * 128:(gt + 1) * 128, :],
                        in_=acc[:, t * 1024:(t + 1) * 1024]).then_inc(S("sOUT").h, 16))
                    S("sOUT").n += 16
                    S("sOUT").mark((cch, t))
                S("sOUT").mark(("chunk", cch))
            P.wait("sync", S("sOUT"), S("sOUT").n)
            P.flush(block)

    return nc


_NC_CACHE = {}


def _get_nc(T):
    if T not in _NC_CACHE:
        _NC_CACHE[T] = build_nc(T)
    return _NC_CACHE[T]


def kernel(L0_attn_x, L0_attn_w_Q, L0_attn_w_K, L0_attn_w_V, L0_ffn_router,
           L0_ffn_experts_gate, L0_ffn_experts_up, L0_ffn_experts_down,
           L0_ffn_shared_gate, L0_ffn_shared_up, L0_ffn_shared_down):
    f32 = lambda a: np.ascontiguousarray(np.asarray(a, dtype=np.float32))
    x = f32(L0_attn_x)
    common = {
        "wq": f32(L0_attn_w_Q), "wk": f32(L0_attn_w_K), "wv": f32(L0_attn_w_V),
        "rt": f32(L0_ffn_router),
        "eg": f32(L0_ffn_experts_gate), "eu": f32(L0_ffn_experts_up),
        "ed": f32(L0_ffn_experts_down),
        "sg": f32(L0_ffn_shared_gate), "su": f32(L0_ffn_shared_up),
        "sd": f32(L0_ffn_shared_down),
        "idf": np.eye(128, dtype=np.float32),
    }
    nc = _get_nc(4096)
    in_maps = [dict(common, x=x[b]) for b in range(8)]
    res = run_bass_kernel_spmd(nc, in_maps, core_ids=list(range(8)))
    return np.stack([res.results[b]["out"] for b in range(8)], axis=0)


# version 12
# speedup vs baseline: 1.4611x; 1.0036x over previous
import contextlib
import numpy as np

import concourse.bass as bass
import concourse.mybir as mybir
import concourse.bass_utils as _bu
from concourse.bass_utils import run_bass_kernel_spmd

_orig_run_command = _bu.run_command


def _run_command_ldwopt(cmd, *a, **kw):
    cmd = [c.replace("--enable-ldw-opt=false", "--enable-ldw-opt=true")
           if isinstance(c, str) else c for c in cmd]
    return _orig_run_command(cmd, *a, **kw)


_bu.run_command = _run_command_ldwopt

F32 = mybir.dt.float32
BF16 = mybir.dt.bfloat16
AF = mybir.ActivationFunctionType
ALU = mybir.AluOpType
AX = mybir.AxisListType

E = 1024
NE = 8
NEXP = 10
EPS = 1e-5


class Sem:
    def __init__(self, stack, nc, name):
        self.h = stack.enter_context(nc.semaphore(name))
        self.n = 0
        self.hist = {}

    def mark(self, key):
        self.hist[key] = self.n

    def at(self, key):
        return self.hist[key]


class Prog:
    ENGS = ("sync", "gpsimd", "tensor", "vector", "scalar")

    def __init__(self):
        self.ops = {e: [] for e in self.ENGS}

    def emit(self, eng, fn):
        self.ops[eng].append(fn)

    def wait(self, eng, sem, val, future=False):
        assert future or val <= sem.n, f"wait {val} > produced {sem.n} ({eng})"
        if val <= 0:
            return
        self.emit(eng, lambda En, s=sem.h, v=val: En.wait_ge(s, v))

    def flush(self, block):
        for eng in self.ENGS:
            lst = self.ops[eng]
            if not lst:
                continue

            def body(En, lst=lst):
                for fn in lst:
                    fn(En)

            getattr(block, eng)(body)
        self.ops = {e: [] for e in self.ENGS}


def build_nc(T=4096, stage="full"):
    NT = T // 128
    NCH = T // 1024
    KT = NT
    NQC = T // 512
    nc = bass.Bass("TRN2", target_bir_lowering=False)

    x_d = nc.dram_tensor("x", [T, E], F32, kind="ExternalInput").ap()
    wq_d = nc.dram_tensor("wq", [E, E], F32, kind="ExternalInput").ap()
    wk_d = nc.dram_tensor("wk", [E, E], F32, kind="ExternalInput").ap()
    wv_d = nc.dram_tensor("wv", [E, E], F32, kind="ExternalInput").ap()
    rt_d = nc.dram_tensor("rt", [E, 8], F32, kind="ExternalInput").ap()
    eg_d = nc.dram_tensor("eg", [8, E, 512], F32, kind="ExternalInput").ap()
    eu_d = nc.dram_tensor("eu", [8, E, 512], F32, kind="ExternalInput").ap()
    ed_d = nc.dram_tensor("ed", [8, 512, E], F32, kind="ExternalInput").ap()
    sg_d = nc.dram_tensor("sg", [E, E], F32, kind="ExternalInput").ap()
    su_d = nc.dram_tensor("su", [E, E], F32, kind="ExternalInput").ap()
    sd_d = nc.dram_tensor("sd", [E, E], F32, kind="ExternalInput").ap()
    idf_d = nc.dram_tensor("idf", [128, 128], F32, kind="ExternalInput").ap()
    out_d = nc.dram_tensor("out", [T, E], F32, kind="ExternalOutput").ap()
    attn_d = nc.dram_tensor("attn_sc", [T, E], F32,
                            kind="Internal" if stage == "full"
                            else "ExternalOutput").ap()
    dbg_d = (nc.dram_tensor("dbg", [128, NE * T], BF16, kind="ExternalOutput").ap()
             if stage in ("ln", "qkv") else None)
    gbf_d = nc.dram_tensor("gbf", [NEXP, 128, 4096], BF16, kind="Internal").ap()
    ubf_d = nc.dram_tensor("ubf", [NEXP, 128, 4096], BF16, kind="Internal").ap()
    dbf_d = nc.dram_tensor("dbf", [NEXP, 128, 4096], BF16, kind="Internal").ap()

    def psrc(e):
        if e < 2:
            h = e * 512
            return (sg_d[:, h:h + 512], su_d[:, h:h + 512], sd_d[h:h + 512, :])
        return (eg_d[e - 2], eu_d[e - 2], ed_d[e - 2])

    stack = contextlib.ExitStack()
    with stack:
        xnt = stack.enter_context(nc.sbuf_tensor("xnt", [128, NE * T], BF16))
        idf = stack.enter_context(nc.sbuf_tensor("idf_sb", [128, 128], F32))
        stat = stack.enter_context(nc.sbuf_tensor("stat", [128, 32], F32))
        st8 = stack.enter_context(nc.sbuf_tensor("st8", [128, 64], F32))
        rt_bf = stack.enter_context(nc.sbuf_tensor("rt_bf", [128, 64], BF16))
        logit = stack.enter_context(nc.sbuf_tensor("logit", [128, NT * 8], F32))
        coef = stack.enter_context(nc.sbuf_tensor("coef", [128, NT * 8], F32))
        ps2 = stack.enter_context(nc.psum_tensor("ps2", [128, 1024], F32))
        ps3 = stack.enter_context(nc.psum_tensor("ps3", [128, 1024], F32))

        sm = {}
        for name in ("sXNT sLNa sLNb sLNc sTR sMISC "
                     "sWP sWC sQKVm sQKVe sVT sVTe sONE "
                     "sS sEXP sPVA sPVB sNDA sNDB sTRA sTRB sDIV sDV2 "
                     "sWCc sRT sRTe sSMv sSMs "
                     "sTEN sSL sMUL sACC sINIT sOUT").split():
            sm[name] = Sem(stack, nc, name)
        for name in ("sX", "sWCf", "sWCo", "sST", "sFW", "sXR"):
            sm[name + "0"] = Sem(stack, nc, name + "0")
            sm[name + "1"] = Sem(stack, nc, name + "1")
        sm["sST2"] = Sem(stack, nc, "sST2")
        sm["sST3"] = Sem(stack, nc, "sST3")
        S = lambda k: sm[k]
        Sp = lambda k, i: sm[k + str(i % 2)]
        SST = lambda g: sm["sST" + str(g % 4)]

        block = stack.enter_context(nc.Block())
        P = Prog()

        def MM(out_ap, lhsT, rhs, start, stop, sem=None):
            def f(En, o=out_ap, a=lhsT, r=rhs, s1=start, s2=stop, s=sem):
                ins = En.matmul(o, a, r, start=s1, stop=s2, skip_group_check=True)
                if s is not None:
                    ins.then_inc(s.h, 1)
            P.emit("tensor", f)
            if sem is not None:
                sem.n += 1

        def TRN(out_ap, in_ap, np_, sem=None):
            def f(En, o=out_ap, i=in_ap, idn=idf[0:np_, 0:np_], s=sem):
                ins = En.transpose(o, i, idn)
                if s is not None:
                    ins.then_inc(s.h, 1)
            P.emit("tensor", f)
            if sem is not None:
                sem.n += 1

        with contextlib.ExitStack() as ph_a:
            psS = [ph_a.enter_context(nc.psum_tensor(f"psS{i}", [128, 1024], F32))
                   for i in range(2)]
            wst = ph_a.enter_context(nc.sbuf_tensor("wst", [128, 1024], F32))
            wpq = ph_a.enter_context(nc.sbuf_tensor("wpq", [128, 2 * 1024], BF16))
            wpk = ph_a.enter_context(nc.sbuf_tensor("wpk", [128, 2 * 1024], BF16))
            wpv = ph_a.enter_context(nc.sbuf_tensor("wpv", [128, 2 * 1024], BF16))
            wcf = ph_a.enter_context(nc.sbuf_tensor("wcf", [128, 2 * 1024], F32))
            wcb = ph_a.enter_context(nc.sbuf_tensor("wcb", [128, 2 * 1024], BF16))
            xt = ph_a.enter_context(nc.sbuf_tensor("xt", [128, 2 * 1024], F32))
            qt = ph_a.enter_context(nc.sbuf_tensor("qt", [128, 2 * T], BF16))
            kt = ph_a.enter_context(nc.sbuf_tensor("kt", [128, 2 * T], BF16))
            vt = ph_a.enter_context(nc.sbuf_tensor("vt", [128, 2 * NT * 129],
                                                   BF16))
            es = ph_a.enter_context(nc.sbuf_tensor("es", [128, 2 * 2048], BF16))
            nd = ph_a.enter_context(nc.sbuf_tensor("nd", [128, 2048], F32))
            astg = ph_a.enter_context(nc.sbuf_tensor("astg", [128, 2 * 1024], F32))
            psP = psS

            P.emit("sync", lambda En: En.dma_start(out=idf[:], in_=idf_d)
                   .then_inc(S("sMISC").h, 16))
            S("sMISC").n += 16
            P.emit("sync", lambda En: En.dma_start(
                out=astg[:, 0:64].rearrange("p (i e) -> p i e", e=8),
                in_=rt_d.rearrange("(i p) e -> p i e", p=128))
                .then_inc(S("sMISC").h, 16))
            S("sMISC").n += 16
            P.wait("vector", S("sMISC"), 32)
            P.emit("vector", lambda En: En.tensor_copy(rt_bf[:], astg[:, 0:64])
                   .then_inc(S("sMISC").h, 1))
            S("sMISC").n += 1
            P.emit("vector", lambda En: En.memset(
                vt[:].rearrange("p (j c) -> p j c", c=129)[:, :, 64:65], 1.0)
                .then_inc(S("sONE").h, 1))
            S("sONE").n += 1

            def wpair_load_cast(p):
                b = p % 2
                for wi, wsrc in enumerate((wq_d, wk_d, wv_d)):
                    prev = 3 * p + wi - 1
                    P.wait("gpsimd", S("sWC"),
                           S("sWC").at(prev) if prev >= 0 else 0)
                    P.emit("gpsimd", lambda En, i=wsrc, p=p: En.dma_start(
                        out=wst[:].rearrange("p (i m) -> p i m", m=128),
                        in_=i[:, p * 128:(p + 1) * 128]
                            .rearrange("(i p) m -> p i m", p=128))
                        .then_inc(S("sWP").h, 16))
                    S("sWP").n += 16
                    S("sWP").mark(3 * p + wi)
                    dstt = (wpq, wpk, wpv)[wi]
                    P.wait("vector", S("sWP"), S("sWP").at(3 * p + wi))
                    if p >= 2 and wi == 0:
                        P.wait("vector", S("sQKVm"),
                               S("sQKVm").at(("pairdone", p - 1)))
                    P.emit("vector", lambda En, o=dstt, b=b: En.tensor_copy(
                        o[:, b * 1024:(b + 1) * 1024], wst[:])
                        .then_inc(S("sWC").h, 1))
                    S("sWC").n += 1
                    S("sWC").mark(3 * p + wi)
                S("sWC").mark(("pair", p))

            wpair_load_cast(0)

            def ln_stage1(j):
                b = j % 2
                s0 = b * 8
                xtb = xt[:, b * 1024:(b + 1) * 1024]
                P.wait("sync", S("sTR"), S("sTR").at(j - 2) if j >= 2 else 0)
                P.emit("sync", lambda En, j=j, o=xtb: En.dma_start(
                    out=o, in_=x_d[j * 128:(j + 1) * 128, :])
                    .then_inc(Sp("sX", j).h, 16))
                Sp("sX", j).n += 16
                Sp("sX", j).mark(j)
                P.wait("vector", Sp("sX", j), Sp("sX", j).at(j))
                P.emit("vector", lambda En, i=xtb, s0=s0: En.reduce_sum(
                    stat[:, s0:s0 + 1], i, axis=AX.X).then_inc(S("sLNa").h, 1))
                S("sLNa").n += 1
                P.wait("vector", S("sLNa"), S("sLNa").n)
                P.emit("vector", lambda En, s0=s0: En.tensor_scalar(
                    stat[:, s0 + 1:s0 + 2], stat[:, s0:s0 + 1], 1.0 / E, None,
                    ALU.mult).then_inc(S("sLNa").h, 1))
                S("sLNa").n += 1
                P.wait("vector", S("sLNa"), S("sLNa").n)
                P.emit("vector", lambda En, i=xtb, s0=s0: En.tensor_scalar(
                    i, i, stat[:, s0 + 1:s0 + 2], None, ALU.subtract)
                    .then_inc(S("sLNa").h, 1))
                S("sLNa").n += 1
                S("sLNa").mark(("xc", j))
                P.wait("scalar", S("sLNa"), S("sLNa").at(("xc", j)))
                if j >= 2:
                    P.wait("scalar", S("sLNa"), S("sLNa").at(("ve", j - 2)))
                if j == 0:
                    P.wait("scalar", S("sMISC"), 33)
                P.emit("scalar", lambda En, i=xtb, b=b, s0=s0: En.activation(
                    astg[:, b * 1024:(b + 1) * 1024], i, AF.Square,
                    accum_out=stat[:, s0 + 2:s0 + 3]).then_inc(S("sLNb").h, 1))
                S("sLNb").n += 1
                S("sLNb").mark(("sq", j))

            def ln_stage2(j):
                b = j % 2
                s0 = b * 8
                xtb = xt[:, b * 1024:(b + 1) * 1024]
                P.wait("vector", S("sLNb"), S("sLNb").at(("sq", j)))
                P.emit("vector", lambda En, s0=s0: En.tensor_scalar(
                    stat[:, s0 + 3:s0 + 4], stat[:, s0 + 2:s0 + 3], 1.0 / E,
                    EPS, ALU.mult, ALU.add).then_inc(S("sLNa").h, 1))
                S("sLNa").n += 1
                S("sLNa").mark(("ve", j))
                P.wait("scalar", S("sLNa"), S("sLNa").at(("ve", j)))
                if j >= 2:
                    P.wait("scalar", S("sLNc"), S("sLNc").at(j - 2))
                P.emit("scalar", lambda En, s0=s0: En.activation(
                    stat[:, s0 + 4:s0 + 5], stat[:, s0 + 3:s0 + 4], AF.Ln)
                    .then_inc(S("sLNb").h, 1))
                S("sLNb").n += 1
                P.wait("scalar", S("sLNb"), S("sLNb").n)
                P.emit("scalar", lambda En, s0=s0: En.activation(
                    stat[:, s0 + 5:s0 + 6], stat[:, s0 + 4:s0 + 5], AF.Exp,
                    scale=-0.5).then_inc(S("sLNb").h, 1))
                S("sLNb").n += 1
                S("sLNb").mark(("rs", j))
                P.wait("vector", S("sLNb"), S("sLNb").at(("rs", j)))
                P.emit("vector", lambda En, i=xtb, s0=s0: En.tensor_scalar(
                    i, i, stat[:, s0 + 5:s0 + 6], None, ALU.mult)
                    .then_inc(S("sLNc").h, 1))
                S("sLNc").n += 1
                S("sLNc").mark(j)
                P.wait("tensor", S("sLNc"), S("sLNc").at(j))
                if j >= 2:
                    P.wait("tensor", S("sXNT"), S("sXNT").at(j - 2))
                if j == 0:
                    P.wait("tensor", S("sMISC"), 32)
                for c in range(8):
                    TRN(psP[b][:, c * 128:(c + 1) * 128],
                        xtb[:, c * 128:(c + 1) * 128],
                        128, sem=S("sTR") if c == 7 else None)
                S("sTR").mark(j)
                P.wait("vector", S("sTR"), S("sTR").at(j))
                P.emit("vector", lambda En, j=j, b=b: En.tensor_copy(
                    xnt[:].rearrange("p (i t) -> p i t", t=T)
                        [:, :, j * 128:(j + 1) * 128],
                    psP[b][:].rearrange("p (i t) -> p i t", t=128))
                    .then_inc(S("sXNT").h, 1))
                S("sXNT").n += 1
                S("sXNT").mark(j)

            for j in range(NT + 1):
                if j < NT:
                    ln_stage1(j)
                if j >= 1:
                    ln_stage2(j - 1)

            if stage == "ln":
                P.wait("sync", S("sXNT"), S("sXNT").at(NT - 1))
                P.emit("sync", lambda En: En.dma_start(out=dbg_d, in_=xnt[:])
                       .then_inc(S("sMISC").h, 16))
                S("sMISC").n += 16
                P.wait("sync", S("sMISC"), S("sMISC").n)
                P.flush(block)
                return nc

            pieces = []
            for e in range(NEXP):
                g, u, d = psrc(e)
                for r in range(4):
                    pieces.append((g[256 * r:256 * (r + 1), :],
                                   gbf_d[e][:, r * 1024:(r + 1) * 1024]
                                   .rearrange("p (i c) -> p i c", c=512)))
                for r in range(4):
                    pieces.append((u[256 * r:256 * (r + 1), :],
                                   ubf_d[e][:, r * 1024:(r + 1) * 1024]
                                   .rearrange("p (i c) -> p i c", c=512)))
                for ch in range(2):
                    for r in range(2):
                        pieces.append((d[256 * r:256 * (r + 1),
                                         512 * ch:512 * (ch + 1)],
                                       dbf_d[e].rearrange("p (k h) -> p k h", h=1024)
                                       [:, 2 * r:2 * r + 2, ch * 512:(ch + 1) * 512]))
            NPC = len(pieces)

            def precast_step(k):
                b = k % 2
                src, dst = pieces[k]
                fbuf = wcf[:, b * 1024:(b + 1) * 1024]
                bbuf = wcb[:, b * 1024:(b + 1) * 1024]
                P.wait("gpsimd", S("sWCc"), S("sWCc").at(k - 2) if k >= 2 else 0)
                P.emit("gpsimd", lambda En, o=fbuf, i=src: En.dma_start(
                    out=o.rearrange("p (i c) -> p i c", c=512),
                    in_=i.rearrange("(i p) c -> p i c", p=128))
                    .then_inc(Sp("sWCf", k).h, 16))
                Sp("sWCf", k).n += 16
                Sp("sWCf", k).mark(k)
                P.wait("vector", Sp("sWCf", k), Sp("sWCf", k).at(k))
                if k >= 2:
                    P.wait("vector", Sp("sWCo", k), Sp("sWCo", k).at(k - 2))
                P.emit("vector", lambda En, o=bbuf, i=fbuf: En.tensor_copy(o, i)
                       .then_inc(S("sWCc").h, 1))
                S("sWCc").n += 1
                S("sWCc").mark(k)
                P.wait("gpsimd", S("sWCc"), S("sWCc").at(k))
                P.emit("gpsimd", lambda En, o=dst, i=bbuf, kk=k: En.dma_start(
                    out=o, in_=i.rearrange("p (i c) -> p i c", c=512))
                    .then_inc(Sp("sWCo", kk).h, 16))
                Sp("sWCo", k).n += 16
                Sp("sWCo", k).mark(k)

            precast_iter = iter(range(NPC))

            NC2 = T // 512

            def att_prev(p, c, j, k):
                g = (p * NC2 + c) * KT + j - k
                if g < 0:
                    return None
                pp, r = divmod(g, NC2 * KT)
                cc, jj = divmod(r, KT)
                return (pp, cc, jj)

            def S_emit(p, c, j):
                q0 = (p % 2) * T + c * 512
                prev = att_prev(p, c, j, 2)
                if prev is not None:
                    P.wait("tensor", S("sEXP"), S("sEXP").at(prev))
                if c == 0 and j == 0:
                    P.wait("tensor", S("sQKVe"), S("sQKVe").at(("allqkv", p)))
                for hi, ktb in ((0, 0), (1, 64)):
                    MM(psS[j % 2][:, hi * 512:(hi + 1) * 512],
                       kt[ktb:ktb + 64, (p % 2) * T + j * 128:
                          (p % 2) * T + (j + 1) * 128],
                       qt[ktb:ktb + 64, q0:q0 + 512],
                       start=True, stop=True,
                       sem=S("sS") if hi == 1 else None)
                S("sS").mark((p, c, j))

            def EXP_emit(p, c, j):
                P.wait("scalar", S("sS"), S("sS").at((p, c, j)))
                prev = att_prev(p, c, j, 4)
                if prev is not None:
                    P.wait("scalar", S("sPVA"), S("sPVA").at(prev))
                P.emit("scalar", lambda En, j4=j % 4, j2=j % 2: En.activation(
                    es[:, j4 * 1024:(j4 + 1) * 1024], psS[j2][:, 0:1024],
                    AF.Exp, scale=0.125).then_inc(S("sEXP").h, 1))
                S("sEXP").n += 1
                S("sEXP").mark((p, c, j))

            def PV_emit(p, c, jj):
                P.wait("tensor", S("sEXP"), S("sEXP").at((p, c, jj)))
                if jj == 0:
                    P.wait("tensor", S("sONE"), 1)
                    P.wait("tensor", S("sVTe"), S("sVTe").at((p, NT - 1)))
                    if c >= 1:
                        P.wait("tensor", S("sNDA"), S("sNDA").at((p, c - 1)))
                    elif p >= 1:
                        P.wait("tensor", S("sNDA"),
                               S("sNDA").at((p - 1, NC2 - 1)))
                vo = (p % 2) * (NT * 129)
                for hi, voff in ((0, 0), (1, 64)):
                    MM(ps2[0:65, hi * 512:(hi + 1) * 512],
                       vt[:, vo + jj * 129 + voff:vo + jj * 129 + voff + 65],
                       es[:, (jj % 4) * 1024 + hi * 512:
                          (jj % 4) * 1024 + (hi + 1) * 512],
                       start=(jj == 0), stop=(jj == KT - 1),
                       sem=S("sPVA") if hi == 1 else None)
                S("sPVA").mark((p, c, jj))

            ps3c = {"A": None, "B": None}

            def chain_wait(half):
                tok = ps3c[half]
                if tok is not None:
                    P.wait("tensor", tok[0], tok[1])

            def qkv_gen(p):
                b = p % 2
                for (wbuf, dst, pi) in ((wpq, qt, 0), (wpk, kt, 1)):
                    for n in range(NQC):
                        nh = n % 2
                        half = "AB"[nh]
                        chain_wait(half)
                        if pi == 0 and n == 0:
                            P.wait("tensor", S("sWC"), S("sWC").at(("pair", p)))
                        if p == 0:
                            P.wait("tensor", S("sXNT"), S("sXNT").at(
                                min(NT - 1, (n + 1) * 4 - 1)))
                        for c in range(NE):
                            MM(ps3[:, nh * 512:(nh + 1) * 512],
                               wbuf[:, b * 1024 + c * 128:
                                    b * 1024 + (c + 1) * 128],
                               xnt[:, c * T + n * 512:c * T + (n + 1) * 512],
                               start=(c == 0), stop=(c == NE - 1),
                               sem=S("sQKVm") if c == NE - 1 else None)
                        S("sQKVm").mark((p, pi, n))
                        P.wait("vector", S("sQKVm"), S("sQKVm").at((p, pi, n)))
                        if p >= 2 and pi == 0 and n == 0:
                            P.wait("vector", S("sS"), S("sS").at(("pair", p - 2)))
                        P.emit("vector", lambda En, o=dst, n=n, nh=nh, b=b:
                               En.tensor_copy(
                            o[:, b * T + n * 512:b * T + (n + 1) * 512],
                            ps3[:, nh * 512:(nh + 1) * 512])
                            .then_inc(S("sQKVe").h, 1))
                        S("sQKVe").n += 1
                        S("sQKVe").mark((p, pi, n))
                        ps3c[half] = (S("sQKVe"), S("sQKVe").n)
                        yield
                for n in range(NQC):
                    nh = n % 2
                    half = "AB"[nh]
                    chain_wait(half)
                    for c in range(NE):
                        MM(ps3[:, nh * 512:(nh + 1) * 512],
                           wpv[:, b * 1024 + c * 128:b * 1024 + (c + 1) * 128],
                           xnt[:, c * T + n * 512:c * T + (n + 1) * 512],
                           start=(c == 0), stop=(c == NE - 1),
                           sem=S("sQKVm") if c == NE - 1 else None)
                    S("sQKVm").mark((p, "v", n))
                    P.wait("vector", S("sQKVm"), S("sQKVm").at((p, "v", n)))
                    vtf = xt[:, nh * 1024:nh * 1024 + 512]
                    if n >= 2:
                        P.wait("vector", S("sVT"), S("sVT").at((p, (n - 2) * 4 + 3)))
                    elif p >= 1:
                        P.wait("vector", S("sVT"), S("sVT").at(("pair", p - 1)))
                    else:
                        P.wait("vector", S("sLNc"), S("sLNc").n)
                    P.emit("vector", lambda En, o=vtf, nh=nh: En.tensor_copy(
                        o, ps3[:, nh * 512:(nh + 1) * 512])
                        .then_inc(S("sQKVe").h, 1))
                    S("sQKVe").n += 1
                    S("sQKVe").mark((p, "v", n))
                    ps3c[half] = (S("sQKVe"), S("sQKVe").n)
                    yield
                    chain_wait(half)
                    P.wait("tensor", S("sQKVe"), S("sQKVe").at((p, "v", n)))
                    for k in range(4):
                        TRN(ps3[:, nh * 512 + k * 128:nh * 512 + (k + 1) * 128],
                            vtf[:, k * 128:(k + 1) * 128],
                            128, sem=S("sVT") if k == 3 else None)
                        S("sVT").mark((p, n * 4 + k))
                    P.wait("vector", S("sVT"), S("sVT").at((p, n * 4 + 3)))
                    if p >= 2 and n == 0:
                        P.wait("vector", S("sPVA"), S("sPVA").at(("pair", p - 2)))
                    for k in range(4):
                        tt = n * 4 + k
                        col = nh * 512 + k * 128
                        vb = b * (NT * 129)
                        P.emit("vector", lambda En, tt=tt, col=col, vb=vb:
                               En.tensor_copy(vt[:, vb + tt * 129:vb + tt * 129 + 64],
                                              ps3[:, col:col + 64]))
                        P.emit("vector", lambda En, tt=tt, col=col, vb=vb:
                               En.tensor_copy(
                            vt[:, vb + tt * 129 + 65:vb + tt * 129 + 129],
                            ps3[:, col + 64:col + 128])
                            .then_inc(S("sVTe").h, 1))
                        S("sVTe").n += 1
                        S("sVTe").mark((p, tt))
                    ps3c[half] = (S("sVTe"), S("sVTe").n)
                    yield
                S("sVT").mark(("pair", p))
                S("sQKVm").mark(("pairdone", p))
                S("sQKVe").mark(("allqkv", p))

            pend = None
            for p in range(8):
                b = p % 2
                if p + 1 < 8:
                    wpair_load_cast(p + 1)
                if p == 0:
                    for _ in qkv_gen(0):
                        pass
                pend = qkv_gen(p + 1) if p + 1 < 8 else None

                if stage == "qkv" and p == 0:
                    P.wait("sync", S("sQKVe"), S("sQKVe").at(("allqkv", 0)))
                    P.wait("sync", S("sVTe"), S("sVTe").at((0, NT - 1)))
                    P.emit("sync", lambda En: En.dma_start(
                        out=dbg_d[:, 0:T], in_=qt[:]).then_inc(S("sMISC").h, 16))
                    S("sMISC").n += 16
                    P.emit("sync", lambda En: En.dma_start(
                        out=dbg_d[:, T:2 * T], in_=kt[:]).then_inc(S("sMISC").h, 16))
                    S("sMISC").n += 16
                    P.emit("sync", lambda En: En.dma_start(
                        out=dbg_d[:, 2 * T:2 * T + NT * 129], in_=vt[:])
                        .then_inc(S("sMISC").h, 16))
                    S("sMISC").n += 16
                    P.wait("sync", S("sMISC"), S("sMISC").n)
                    P.flush(block)
                    return nc

                for c in range(NC2):
                    pre = 2 if c > 0 else 0
                    for j in range(KT):
                        if j >= pre:
                            S_emit(p, c, j)
                        EXP_emit(p, c, j)
                        if j >= 2:
                            PV_emit(p, c, j - 2)
                        if j % 8 == 7 and pend is not None:
                            next(pend, None)
                    PV_emit(p, c, KT - 2)
                    PV_emit(p, c, KT - 1)
                    if c + 1 < NC2:
                        S_emit(p, c + 1, 0)
                        S_emit(p, c + 1, 1)
                    gidx = p * NC2 + c
                    cs = gidx % 4
                    stg = astg[:, cs * 512:(cs + 1) * 512]
                    P.wait("vector", S("sPVA"), S("sPVA").at((p, c, KT - 1)))
                    P.emit("vector", lambda En: En.tensor_copy(
                        nd[0:65, 0:1024], ps2[0:65, 0:1024])
                        .then_inc(S("sNDA").h, 1))
                    S("sNDA").n += 1
                    S("sNDA").mark((p, c))
                    P.wait("tensor", S("sNDA"), S("sNDA").at((p, c)))
                    chain_wait("A")
                    chain_wait("B")
                    for hi, ndo in ((0, 0), (1, 512)):
                        for t in range(4):
                            TRN(ps3[:, ndo + t * 128:ndo + t * 128 + 65],
                                nd[0:65, ndo + t * 128:ndo + (t + 1) * 128],
                                65, sem=S("sTRA") if (hi == 1 and t == 3)
                                else None)
                    S("sTRA").mark((p, c))
                    P.wait("vector", S("sTRA"), S("sTRA").at((p, c)))
                    P.emit("vector", lambda En: En.tensor_copy(
                        nd[:, 0:1024], ps3[:, 0:1024])
                        .then_inc(S("sDIV").h, 1))
                    S("sDIV").n += 1
                    S("sDIV").mark((p, c))
                    ps3c["A"] = (S("sDIV"), S("sDIV").n)
                    ps3c["B"] = (S("sDIV"), S("sDIV").n)
                    for (hh, ndo) in (("A", 0), ("B", 512)):
                        dcol = 64 if hh == "A" else 0
                        ncol0 = 0 if hh == "A" else 1
                        scol = 0 if hh == "A" else 64
                        if hh == "A":
                            P.wait("vector", S("sDIV"), S("sDIV").at((p, c)))
                        if hh == "A" and gidx >= 4:
                            P.wait("vector", SST(gidx), SST(gidx).at(gidx - 4))
                        for t in range(4):
                            if not (hh == "A" and t == 0):
                                P.wait("vector", S("sDV2"), S("sDV2").n)
                            P.emit("vector", lambda En, ndo=ndo, t=t, dcol=dcol:
                                   En.reciprocal(
                                stat[:, 16:17],
                                nd[:, ndo + t * 128 + dcol:
                                   ndo + t * 128 + dcol + 1])
                                .then_inc(S("sSMv").h, 1))
                            S("sSMv").n += 1
                            P.wait("vector", S("sSMv"), S("sSMv").n)
                            P.emit("vector", lambda En, ndo=ndo, t=t, n0=ncol0,
                                   sc=scol, stg=stg: En.tensor_scalar(
                                stg[:, t * 128 + sc:t * 128 + sc + 64],
                                nd[:, ndo + t * 128 + n0:ndo + t * 128 + n0 + 64],
                                stat[:, 16:17], None, ALU.mult)
                                .then_inc(S("sDV2").h, 1))
                            S("sDV2").n += 1
                    S("sDV2").mark((p, c))
                    P.wait("sync", S("sDV2"), S("sDV2").at((p, c)))
                    P.emit("sync", lambda En, p=p, c=c, stg=stg,
                           sh=SST(gidx).h: En.dma_start(
                        out=attn_d.rearrange("(J p) e -> p J e", p=128)
                            [:, c * 4:(c + 1) * 4, p * 128:(p + 1) * 128],
                        in_=stg.rearrange("p (J e) -> p J e", e=128))
                        .then_inc(sh, 16))
                    SST(gidx).n += 16
                    SST(gidx).mark(gidx)
                    for _ in range(2):
                        k = next(precast_iter, None)
                        if k is not None:
                            precast_step(k)
                if pend is not None:
                    for _ in pend:
                        pass
                S("sEXP").mark(("pair", p))
                S("sS").mark(("pair", p))
                S("sPVA").mark(("pair", p))
            for k in precast_iter:
                precast_step(k)
            for i2 in range(4):
                SST(i2).mark("all")
            for i2 in range(2):
                Sp("sWCo", i2).mark("all")
            S("sDIV").mark("attn_done")
            S("sEXP").mark("attn_done")

            if stage == "attn":
                for i2 in range(4):
                    P.wait("sync", SST(i2), SST(i2).at("all"))
                for i2 in range(2):
                    P.wait("sync", Sp("sWCo", i2), Sp("sWCo", i2).at("all"))
                P.flush(block)
                return nc

            P.flush(block)

        with contextlib.ExitStack() as ph_f:
            psD = ph_f.enter_context(nc.psum_tensor("psD", [128, 2048], F32))
            acc = ph_f.enter_context(nc.sbuf_tensor("acc", [128, 8 * 1024], F32))
            wg = ph_f.enter_context(nc.sbuf_tensor("wg", [128, 2 * 4096], BF16))
            wu = ph_f.enter_context(nc.sbuf_tensor("wu", [128, 2 * 4096], BF16))
            wd = ph_f.enter_context(nc.sbuf_tensor("wd", [128, 2 * 4096], BF16))
            m_sb = ph_f.enter_context(nc.sbuf_tensor("m_sb", [128, 2 * 4096], BF16))
            xres = ph_f.enter_context(nc.sbuf_tensor("xres", [128, 2 * 1024], F32))
            ares = ph_f.enter_context(nc.sbuf_tensor("ares", [128, 2 * 1024], F32))
            sl = ph_f.enter_context(nc.sbuf_tensor("sl", [128, 1024], F32))

            def phase_gate(eng):
                for i2 in range(2):
                    P.wait(eng, Sp("sWCo", i2), Sp("sWCo", i2).at("all"))
                for i2 in range(4):
                    P.wait(eng, SST(i2), SST(i2).at("all"))
                P.wait(eng, S("sDIV"), S("sDIV").at("attn_done"))
                P.wait(eng, S("sEXP"), S("sEXP").at("attn_done"))
                P.wait(eng, S("sWC"), S("sWC").n)
                P.wait(eng, S("sVTe"), S("sVTe").n)
                P.wait(eng, S("sQKVe"), S("sQKVe").n)
            for eng in ("tensor", "vector", "scalar", "sync", "gpsimd"):
                phase_gate(eng)
            for g in range(NT // 4):
                cb = (g % 2) * 1024
                if g >= 2:
                    P.wait("tensor", S("sRTe"), S("sRTe").at(g - 2))
                for t4 in range(4):
                    tt = g * 4 + t4
                    for i in range(NE):
                        MM(psD[:, cb + t4 * 128:cb + t4 * 128 + 8],
                           xnt[:, i * T + tt * 128:i * T + (tt + 1) * 128],
                           rt_bf[:, i * 8:(i + 1) * 8],
                           start=(i == 0), stop=(i == NE - 1),
                           sem=S("sRT") if (t4 == 3 and i == NE - 1) else None)
                S("sRT").mark(g)
                P.wait("vector", S("sRT"), S("sRT").at(g))
                P.emit("vector", lambda En, cb=cb, g=g: En.tensor_copy(
                    logit[:, g * 32:(g + 1) * 32].rearrange("p (t r) -> p t r", r=8),
                    psD[:].rearrange("p (t r) -> p t r", r=128)
                        [:, 8 * (cb // 1024):8 * (cb // 1024) + 4, 0:8])
                    .then_inc(S("sRTe").h, 1))
                S("sRTe").n += 1
                S("sRTe").mark(g)
            S("sRTe").mark("all")
            for tt in range(NT):
                L = logit[:, tt * 8:(tt + 1) * 8]
                P.wait("vector", S("sRTe"), S("sRTe").at(tt // 4))
                c0, c1, c2, c3, c4 = (stat[:, 9:10], stat[:, 10:11], stat[:, 11:12],
                                      stat[:, 12:13], stat[:, 13:14])
                pbuf, ge1, msk, ge2 = (st8[:, 0:8], st8[:, 8:16],
                                       st8[:, 16:24], st8[:, 24:32])
                P.emit("vector", lambda En, L=L, o=c0: En.reduce_max(
                    o, L, axis=AX.X, negate=True).then_inc(S("sSMv").h, 1))
                S("sSMv").n += 1
                P.emit("vector", lambda En, L=L, o=c1: En.reduce_max(
                    o, L, axis=AX.X).then_inc(S("sSMv").h, 1))
                S("sSMv").n += 1
                P.wait("scalar", S("sSMv"), S("sSMv").n)
                P.emit("scalar", lambda En, L=L, o=pbuf, bb=c0, a=c2: En.activation(
                    o, L, AF.Exp, bias=bb, accum_out=a).then_inc(S("sSMs").h, 1))
                S("sSMs").n += 1
                S("sSMs").mark(("exp", tt))
                P.wait("vector", S("sSMs"), S("sSMs").at(("exp", tt)))
                P.emit("vector", lambda En, L=L, o=ge1, m=c1: En.tensor_single_scalar(
                    o, L, m, ALU.is_ge).then_inc(S("sSMv").h, 1))
                S("sSMv").n += 1
                P.wait("vector", S("sSMv"), S("sSMv").n)
                P.emit("vector", lambda En, L=L, o=msk, g1=ge1: En.scalar_tensor_tensor(
                    o, g1, -1e30, L, ALU.mult, ALU.add).then_inc(S("sSMv").h, 1))
                S("sSMv").n += 1
                P.wait("vector", S("sSMv"), S("sSMv").n)
                P.emit("vector", lambda En, o=c3, m=msk: En.reduce_max(
                    o, m, axis=AX.X).then_inc(S("sSMv").h, 1))
                S("sSMv").n += 1
                P.wait("vector", S("sSMv"), S("sSMv").n)
                P.emit("vector", lambda En, L=L, o=ge2, m=c3: En.tensor_single_scalar(
                    o, L, m, ALU.is_ge).then_inc(S("sSMv").h, 1))
                S("sSMv").n += 1
                P.emit("vector", lambda En, o=c4, i=c2: En.reciprocal(o, i)
                       .then_inc(S("sSMv").h, 1))
                S("sSMv").n += 1
                P.wait("vector", S("sSMv"), S("sSMv").n)
                P.emit("vector", lambda En, o=pbuf, r=c4: En.tensor_scalar(
                    o, o, r, None, ALU.mult).then_inc(S("sSMv").h, 1))
                S("sSMv").n += 1
                P.wait("vector", S("sSMv"), S("sSMv").n)
                P.emit("vector", lambda En, o=coef[:, tt * 8:(tt + 1) * 8],
                       a=pbuf, g2=ge2: En.tensor_mul(o, a, g2)
                       .then_inc(S("sSMv").h, 1))
                S("sSMv").n += 1
            S("sSMv").mark("coef_all")

            def ffn_wload(idx, e):
                eb = idx % 2
                for eng in ("sync", "scalar"):
                    P.wait(eng, Sp("sWCo", 0), Sp("sWCo", 0).at(12 * e + 10))
                    P.wait(eng, Sp("sWCo", 1), Sp("sWCo", 1).at(12 * e + 11))
                    if idx >= 2:
                        P.wait(eng, S("sTEN"), S("sTEN").at(("edone", idx - 2)))
                for (eng, wdst, wsrc) in (("sync", wg, gbf_d),
                                          ("scalar", wu, ubf_d),
                                          ("scalar", wd, dbf_d)):
                    P.emit(eng, lambda En, e=e, eb=eb, o=wdst, i=wsrc,
                           sh=Sp("sFW", idx).h: En.dma_start(
                        out=o[:, eb * 4096:(eb + 1) * 4096], in_=i[e])
                        .then_inc(sh, 16))
                    Sp("sFW", idx).n += 16
                Sp("sFW", idx).mark(idx)

            ffn_wload(0, 0)
            ffn_wload(1, 1)
            for cch in range(NCH):
                for t in range(8):
                    tb = t % 2
                    gt = cch * 8 + t
                    if t >= 2:
                        P.wait("sync", S("sINIT"), S("sINIT").at((cch, t - 2)))
                    elif cch >= 1:
                        P.wait("sync", S("sINIT"), S("sINIT").at((cch - 1, 6 + t)))
                    for i2 in range(4):
                        P.wait("sync", SST(i2), SST(i2).at("all"))
                    P.emit("sync", lambda En, gt=gt, tb=tb, sh=Sp("sXR", t).h:
                           En.dma_start(
                        out=xres[:, tb * 1024:(tb + 1) * 1024],
                        in_=x_d[gt * 128:(gt + 1) * 128, :]).then_inc(sh, 16))
                    Sp("sXR", t).n += 16
                    P.emit("sync", lambda En, gt=gt, tb=tb, sh=Sp("sXR", t).h:
                           En.dma_start(
                        out=ares[:, tb * 1024:(tb + 1) * 1024],
                        in_=attn_d[gt * 128:(gt + 1) * 128, :]).then_inc(sh, 16))
                    Sp("sXR", t).n += 16
                    Sp("sXR", t).mark((cch, t))
                    P.wait("vector", Sp("sXR", t), Sp("sXR", t).at((cch, t)))
                    if cch >= 1:
                        P.wait("vector", S("sOUT"), S("sOUT").at(("chunk", cch - 1)))
                    P.emit("vector", lambda En, t=t, tb=tb: En.tensor_add(
                        acc[:, t * 1024:(t + 1) * 1024],
                        xres[:, tb * 1024:(tb + 1) * 1024],
                        ares[:, tb * 1024:(tb + 1) * 1024]).then_inc(S("sINIT").h, 1))
                    S("sINIT").n += 1
                    S("sINIT").mark((cch, t))

                for e in range(NEXP):
                    idx = cch * NEXP + e
                    eb = idx % 2
                    me = idx % 2
                    for k in range(4):
                        for (wsrc, psb, nm) in ((wg, ps2, "g"), (wu, ps3, "u")):
                            if nm == "g":
                                if k >= 1:
                                    P.wait("tensor", S("sSL"), S("sSL").at((idx, k - 1)))
                                elif idx >= 1:
                                    P.wait("tensor", S("sSL"), S("sSL").at((idx - 1, 3)))
                                elif stage == "full":
                                    pass
                            else:
                                if k >= 1:
                                    P.wait("tensor", S("sMUL"),
                                           S("sMUL").at((idx, k - 1)))
                                elif idx >= 1:
                                    P.wait("tensor", S("sMUL"),
                                           S("sMUL").at((idx - 1, 3)))
                            if k == 0 and nm == "g":
                                P.wait("tensor", Sp("sFW", idx), Sp("sFW", idx).at(idx))
                            for i in range(NE):
                                for hf in range(2):
                                    MM(psb[:, hf * 512:(hf + 1) * 512],
                                       wsrc[:, eb * 4096 + i * 512 + k * 128:
                                            eb * 4096 + i * 512 + (k + 1) * 128],
                                       xnt[:, i * T + cch * 1024 + hf * 512:
                                           i * T + cch * 1024 + (hf + 1) * 512],
                                       start=(i == 0), stop=(i == NE - 1),
                                       sem=S("sTEN") if (i == NE - 1 and hf == 1)
                                       else None)
                            S("sTEN").mark((idx, nm, k))
                        P.wait("scalar", S("sTEN"), S("sTEN").at((idx, "g", k)))
                        if k >= 1:
                            P.wait("scalar", S("sMUL"), S("sMUL").at((idx, k - 1)))
                        elif idx >= 1:
                            P.wait("scalar", S("sMUL"), S("sMUL").at((idx - 1, 3)))
                        P.emit("scalar", lambda En: En.activation(
                            sl[:], ps2[:, 0:1024], AF.Silu).then_inc(S("sSL").h, 1))
                        S("sSL").n += 1
                        S("sSL").mark((idx, k))
                        P.wait("vector", S("sSL"), S("sSL").at((idx, k)))
                        P.wait("vector", S("sTEN"), S("sTEN").at((idx, "u", k)))
                        if idx >= 2 and k == 0:
                            P.wait("vector", S("sTEN"), S("sTEN").at(("edone", idx - 2)))
                        P.emit("vector", lambda En, me=me, k=k: En.tensor_mul(
                            m_sb[:, me * 4096 + k * 1024:me * 4096 + (k + 1) * 1024],
                            sl[:], ps3[:, 0:1024]).then_inc(S("sMUL").h, 1))
                        S("sMUL").n += 1
                        S("sMUL").mark((idx, k))
                    for t in range(8):
                        pcol = (t % 2) * 1024
                        if t >= 2:
                            P.wait("tensor", S("sACC"), S("sACC").at((idx, t - 2)))
                        elif idx >= 1:
                            P.wait("tensor", S("sACC"), S("sACC").at((idx - 1, 6 + t)))
                        else:
                            P.wait("tensor", S("sRTe"), S("sRTe").at("all"))
                        if t == 0:
                            P.wait("tensor", S("sMUL"), S("sMUL").at((idx, 3)))
                        for k in range(4):
                            for hf in range(2):
                                MM(psD[:, pcol + hf * 512:pcol + (hf + 1) * 512],
                                   m_sb[:, me * 4096 + k * 1024 + t * 128:
                                        me * 4096 + k * 1024 + (t + 1) * 128],
                                   wd[:, eb * 4096 + k * 1024 + hf * 512:
                                      eb * 4096 + k * 1024 + (hf + 1) * 512],
                                   start=(k == 0), stop=(k == 3),
                                   sem=S("sTEN") if (hf == 1 and k == 3) else None)
                        S("sTEN").mark((idx, "d", t))
                        P.wait("vector", S("sTEN"), S("sTEN").at((idx, "d", t)))
                        P.wait("vector", S("sINIT"), S("sINIT").at((cch, t)))
                        if e >= 2:
                            P.wait("vector", S("sSMv"), S("sSMv").at("coef_all"))
                        if e >= 1:
                            P.wait("vector", S("sACC"), S("sACC").at((idx - 1, t)))
                        gt = cch * 8 + t
                        if e < 2:
                            P.emit("vector", lambda En, pcol=pcol, t=t: En.tensor_add(
                                acc[:, t * 1024:(t + 1) * 1024],
                                acc[:, t * 1024:(t + 1) * 1024],
                                psD[:, pcol:pcol + 1024]).then_inc(S("sACC").h, 1))
                        else:
                            P.emit("vector", lambda En, pcol=pcol, t=t, gt=gt, e=e:
                                   En.scalar_tensor_tensor(
                                acc[:, t * 1024:(t + 1) * 1024],
                                psD[:, pcol:pcol + 1024],
                                coef[:, gt * 8 + (e - 2):gt * 8 + (e - 2) + 1],
                                acc[:, t * 1024:(t + 1) * 1024],
                                ALU.mult, ALU.add).then_inc(S("sACC").h, 1))
                        S("sACC").n += 1
                        S("sACC").mark((idx, t))
                    S("sTEN").mark(("edone", idx))
                    if idx + 2 < NCH * NEXP:
                        ffn_wload(idx + 2, (idx + 2) % NEXP)
                for t in range(8):
                    gt = cch * 8 + t
                    P.wait("gpsimd", S("sACC"),
                           S("sACC").at((cch * NEXP + NEXP - 1, t)))
                    P.emit("gpsimd", lambda En, gt=gt, t=t: En.dma_start(
                        out=out_d[gt * 128:(gt + 1) * 128, :],
                        in_=acc[:, t * 1024:(t + 1) * 1024]).then_inc(S("sOUT").h, 16))
                    S("sOUT").n += 16
                    S("sOUT").mark((cch, t))
                S("sOUT").mark(("chunk", cch))
            P.wait("sync", S("sOUT"), S("sOUT").n)
            P.flush(block)

    return nc


_NC_CACHE = {}


def _get_nc(T):
    if T not in _NC_CACHE:
        _NC_CACHE[T] = build_nc(T)
    return _NC_CACHE[T]


def kernel(L0_attn_x, L0_attn_w_Q, L0_attn_w_K, L0_attn_w_V, L0_ffn_router,
           L0_ffn_experts_gate, L0_ffn_experts_up, L0_ffn_experts_down,
           L0_ffn_shared_gate, L0_ffn_shared_up, L0_ffn_shared_down):
    f32 = lambda a: np.ascontiguousarray(np.asarray(a, dtype=np.float32))
    x = f32(L0_attn_x)
    common = {
        "wq": f32(L0_attn_w_Q), "wk": f32(L0_attn_w_K), "wv": f32(L0_attn_w_V),
        "rt": f32(L0_ffn_router),
        "eg": f32(L0_ffn_experts_gate), "eu": f32(L0_ffn_experts_up),
        "ed": f32(L0_ffn_experts_down),
        "sg": f32(L0_ffn_shared_gate), "su": f32(L0_ffn_shared_up),
        "sd": f32(L0_ffn_shared_down),
        "idf": np.eye(128, dtype=np.float32),
    }
    nc = _get_nc(4096)
    in_maps = [dict(common, x=x[b]) for b in range(8)]
    res = run_bass_kernel_spmd(nc, in_maps, core_ids=list(range(8)))
    return np.stack([res.results[b]["out"] for b in range(8)], axis=0)


# version 13
# speedup vs baseline: 1.4997x; 1.0264x over previous
import contextlib
import numpy as np

import concourse.bass as bass
import concourse.mybir as mybir
import concourse.bass_utils as _bu
from concourse.bass_utils import run_bass_kernel_spmd

_orig_run_command = _bu.run_command


def _run_command_ldwopt(cmd, *a, **kw):
    cmd = [c.replace("--enable-ldw-opt=false", "--enable-ldw-opt=true")
           if isinstance(c, str) else c for c in cmd]
    return _orig_run_command(cmd, *a, **kw)


_bu.run_command = _run_command_ldwopt

F32 = mybir.dt.float32
BF16 = mybir.dt.bfloat16
AF = mybir.ActivationFunctionType
ALU = mybir.AluOpType
AX = mybir.AxisListType

E = 1024
NE = 8
NEXP = 10
EPS = 1e-5


class Sem:
    def __init__(self, stack, nc, name):
        self.h = stack.enter_context(nc.semaphore(name))
        self.n = 0
        self.hist = {}

    def mark(self, key):
        self.hist[key] = self.n

    def at(self, key):
        return self.hist[key]


class Prog:
    ENGS = ("sync", "gpsimd", "tensor", "vector", "scalar")

    def __init__(self):
        self.ops = {e: [] for e in self.ENGS}

    def emit(self, eng, fn):
        self.ops[eng].append(fn)

    def wait(self, eng, sem, val, future=False):
        assert future or val <= sem.n, f"wait {val} > produced {sem.n} ({eng})"
        if val <= 0:
            return
        self.emit(eng, lambda En, s=sem.h, v=val: En.wait_ge(s, v))

    def flush(self, block):
        for eng in self.ENGS:
            lst = self.ops[eng]
            if not lst:
                continue

            def body(En, lst=lst):
                for fn in lst:
                    fn(En)

            getattr(block, eng)(body)
        self.ops = {e: [] for e in self.ENGS}


def build_nc(T=4096, stage="full"):
    NT = T // 128
    NCH = T // 1024
    KT = NT
    NQC = T // 512
    nc = bass.Bass("TRN2", target_bir_lowering=False)

    x_d = nc.dram_tensor("x", [T, E], F32, kind="ExternalInput").ap()
    wq_d = nc.dram_tensor("wq", [E, E], F32, kind="ExternalInput").ap()
    wk_d = nc.dram_tensor("wk", [E, E], F32, kind="ExternalInput").ap()
    wv_d = nc.dram_tensor("wv", [E, E], F32, kind="ExternalInput").ap()
    rt_d = nc.dram_tensor("rt", [E, 8], F32, kind="ExternalInput").ap()
    eg_d = nc.dram_tensor("eg", [8, E, 512], F32, kind="ExternalInput").ap()
    eu_d = nc.dram_tensor("eu", [8, E, 512], F32, kind="ExternalInput").ap()
    ed_d = nc.dram_tensor("ed", [8, 512, E], F32, kind="ExternalInput").ap()
    sg_d = nc.dram_tensor("sg", [E, E], F32, kind="ExternalInput").ap()
    su_d = nc.dram_tensor("su", [E, E], F32, kind="ExternalInput").ap()
    sd_d = nc.dram_tensor("sd", [E, E], F32, kind="ExternalInput").ap()
    idf_d = nc.dram_tensor("idf", [128, 128], F32, kind="ExternalInput").ap()
    out_d = nc.dram_tensor("out", [T, E], F32, kind="ExternalOutput").ap()
    attn_d = nc.dram_tensor("attn_sc", [T, E], F32,
                            kind="Internal" if stage == "full"
                            else "ExternalOutput").ap()
    dbg_d = (nc.dram_tensor("dbg", [128, NE * T], BF16, kind="ExternalOutput").ap()
             if stage in ("ln", "qkv") else None)
    gbf_d = nc.dram_tensor("gbf", [NEXP, 128, 4096], BF16, kind="Internal").ap()
    ubf_d = nc.dram_tensor("ubf", [NEXP, 128, 4096], BF16, kind="Internal").ap()
    dbf_d = nc.dram_tensor("dbf", [NEXP, 128, 4096], BF16, kind="Internal").ap()

    def psrc(e):
        if e < 2:
            h = e * 512
            return (sg_d[:, h:h + 512], su_d[:, h:h + 512], sd_d[h:h + 512, :])
        return (eg_d[e - 2], eu_d[e - 2], ed_d[e - 2])

    stack = contextlib.ExitStack()
    with stack:
        xnt = stack.enter_context(nc.sbuf_tensor("xnt", [128, NE * T], BF16))
        idf = stack.enter_context(nc.sbuf_tensor("idf_sb", [128, 128], F32))
        stat = stack.enter_context(nc.sbuf_tensor("stat", [128, 32], F32))
        st8 = stack.enter_context(nc.sbuf_tensor("st8", [128, 64], F32))
        rt_bf = stack.enter_context(nc.sbuf_tensor("rt_bf", [128, 64], BF16))
        logit = stack.enter_context(nc.sbuf_tensor("logit", [128, NT * 8], F32))
        coef = stack.enter_context(nc.sbuf_tensor("coef", [128, NT * 8], F32))
        ps2 = stack.enter_context(nc.psum_tensor("ps2", [128, 1024], F32))
        ps3 = stack.enter_context(nc.psum_tensor("ps3", [128, 1024], F32))

        sm = {}
        for name in ("sXNT sLNa sLNb sLNc sTR sMISC "
                     "sWP sWC sQKVm sQKVe sVT sVTe sONE "
                     "sS sEXP sPVA sPVB sNDA sNDB sTRA sTRB sDIV sDV2 "
                     "sWCc sRT sRTe sSMv sSMs "
                     "sTEN sSL sMUL sACC sINIT sOUT").split():
            sm[name] = Sem(stack, nc, name)
        for name in ("sX", "sWCf", "sWCo", "sST", "sFW", "sXR"):
            sm[name + "0"] = Sem(stack, nc, name + "0")
            sm[name + "1"] = Sem(stack, nc, name + "1")
        sm["sST2"] = Sem(stack, nc, "sST2")
        sm["sST3"] = Sem(stack, nc, "sST3")
        sm["sX2"] = Sem(stack, nc, "sX2")
        sm["sX3"] = Sem(stack, nc, "sX3")
        S = lambda k: sm[k]
        Sp = lambda k, i: sm[k + str(i % 2)]
        SST = lambda g: sm["sST" + str(g % 4)]
        SX = lambda j: sm["sX" + str(j % 4)]

        block = stack.enter_context(nc.Block())
        P = Prog()

        def MM(out_ap, lhsT, rhs, start, stop, sem=None):
            def f(En, o=out_ap, a=lhsT, r=rhs, s1=start, s2=stop, s=sem):
                ins = En.matmul(o, a, r, start=s1, stop=s2, skip_group_check=True)
                if s is not None:
                    ins.then_inc(s.h, 1)
            P.emit("tensor", f)
            if sem is not None:
                sem.n += 1

        def TRN(out_ap, in_ap, np_, sem=None):
            def f(En, o=out_ap, i=in_ap, idn=idf[0:np_, 0:np_], s=sem):
                ins = En.transpose(o, i, idn)
                if s is not None:
                    ins.then_inc(s.h, 1)
            P.emit("tensor", f)
            if sem is not None:
                sem.n += 1

        with contextlib.ExitStack() as ph_a:
            psS = [ph_a.enter_context(nc.psum_tensor(f"psS{i}", [128, 1024], F32))
                   for i in range(2)]
            wst = ph_a.enter_context(nc.sbuf_tensor("wst", [128, 1024], F32))
            wpq = ph_a.enter_context(nc.sbuf_tensor("wpq", [128, 2 * 1024], BF16))
            wpk = ph_a.enter_context(nc.sbuf_tensor("wpk", [128, 2 * 1024], BF16))
            wpv = ph_a.enter_context(nc.sbuf_tensor("wpv", [128, 2 * 1024], BF16))
            wcf = ph_a.enter_context(nc.sbuf_tensor("wcf", [128, 2 * 1024], F32))
            wcb = ph_a.enter_context(nc.sbuf_tensor("wcb", [128, 2 * 1024], BF16))
            xt = ph_a.enter_context(nc.sbuf_tensor("xt", [128, 4 * 1024], F32))
            qt = ph_a.enter_context(nc.sbuf_tensor("qt", [128, 2 * T], BF16))
            kt = ph_a.enter_context(nc.sbuf_tensor("kt", [128, 2 * T], BF16))
            vt = ph_a.enter_context(nc.sbuf_tensor("vt", [128, 2 * NT * 129],
                                                   BF16))
            es = ph_a.enter_context(nc.sbuf_tensor("es", [128, 2 * 2048], BF16))
            nd = ph_a.enter_context(nc.sbuf_tensor("nd", [128, 2048], F32))
            astg = ph_a.enter_context(nc.sbuf_tensor("astg", [128, 2 * 1024], F32))
            psP = psS

            P.emit("sync", lambda En: En.dma_start(out=idf[:], in_=idf_d)
                   .then_inc(S("sMISC").h, 16))
            S("sMISC").n += 16
            P.emit("sync", lambda En: En.dma_start(
                out=astg[:, 0:64].rearrange("p (i e) -> p i e", e=8),
                in_=rt_d.rearrange("(i p) e -> p i e", p=128))
                .then_inc(S("sMISC").h, 16))
            S("sMISC").n += 16
            P.wait("vector", S("sMISC"), 32)
            P.emit("vector", lambda En: En.tensor_copy(rt_bf[:], astg[:, 0:64])
                   .then_inc(S("sMISC").h, 1))
            S("sMISC").n += 1
            P.emit("vector", lambda En: En.memset(
                vt[:].rearrange("p (j c) -> p j c", c=129)[:, :, 64:65], 1.0)
                .then_inc(S("sONE").h, 1))
            S("sONE").n += 1

            def wpair_load_cast(p):
                b = p % 2
                for wi, wsrc in enumerate((wq_d, wk_d, wv_d)):
                    prev = 3 * p + wi - 1
                    P.wait("gpsimd", S("sWC"),
                           S("sWC").at(prev) if prev >= 0 else 0)
                    P.emit("gpsimd", lambda En, i=wsrc, p=p: En.dma_start(
                        out=wst[:].rearrange("p (i m) -> p i m", m=128),
                        in_=i[:, p * 128:(p + 1) * 128]
                            .rearrange("(i p) m -> p i m", p=128))
                        .then_inc(S("sWP").h, 16))
                    S("sWP").n += 16
                    S("sWP").mark(3 * p + wi)
                    dstt = (wpq, wpk, wpv)[wi]
                    P.wait("vector", S("sWP"), S("sWP").at(3 * p + wi))
                    if p >= 2 and wi == 0:
                        P.wait("vector", S("sQKVm"),
                               S("sQKVm").at(("pairdone", p - 1)))
                    P.emit("vector", lambda En, o=dstt, b=b: En.tensor_copy(
                        o[:, b * 1024:(b + 1) * 1024], wst[:])
                        .then_inc(S("sWC").h, 1))
                    S("sWC").n += 1
                    S("sWC").mark(3 * p + wi)
                S("sWC").mark(("pair", p))

            wpair_load_cast(0)

            def ln_stage1(j):
                b = j % 4
                s0 = b * 8
                xtb = xt[:, b * 1024:(b + 1) * 1024]
                P.wait("sync", S("sTR"), S("sTR").at(j - 4) if j >= 4 else 0)
                P.emit("sync", lambda En, j=j, o=xtb, sh=SX(j).h: En.dma_start(
                    out=o, in_=x_d[j * 128:(j + 1) * 128, :])
                    .then_inc(sh, 16))
                SX(j).n += 16
                SX(j).mark(j)
                P.wait("vector", SX(j), SX(j).at(j))
                P.emit("vector", lambda En, i=xtb, s0=s0: En.reduce_sum(
                    stat[:, s0:s0 + 1], i, axis=AX.X).then_inc(S("sLNa").h, 1))
                S("sLNa").n += 1
                P.wait("vector", S("sLNa"), S("sLNa").n)
                P.emit("vector", lambda En, s0=s0: En.tensor_scalar(
                    stat[:, s0 + 1:s0 + 2], stat[:, s0:s0 + 1], 1.0 / E, None,
                    ALU.mult).then_inc(S("sLNa").h, 1))
                S("sLNa").n += 1
                P.wait("vector", S("sLNa"), S("sLNa").n)
                P.emit("vector", lambda En, i=xtb, s0=s0: En.tensor_scalar(
                    i, i, stat[:, s0 + 1:s0 + 2], None, ALU.subtract)
                    .then_inc(S("sLNa").h, 1))
                S("sLNa").n += 1
                S("sLNa").mark(("xc", j))
                P.wait("scalar", S("sLNa"), S("sLNa").at(("xc", j)))
                if j >= 4:
                    P.wait("scalar", S("sLNa"), S("sLNa").at(("ve", j - 4)))
                if j == 0:
                    P.wait("scalar", S("sMISC"), 33)
                P.emit("scalar", lambda En, i=xtb, b2=b % 2, s0=s0: En.activation(
                    astg[:, b2 * 1024:(b2 + 1) * 1024], i, AF.Square,
                    accum_out=stat[:, s0 + 2:s0 + 3]).then_inc(S("sLNb").h, 1))
                S("sLNb").n += 1
                S("sLNb").mark(("sq", j))

            def ln_stage2(j):
                b = j % 4
                s0 = b * 8
                xtb = xt[:, b * 1024:(b + 1) * 1024]
                P.wait("vector", S("sLNb"), S("sLNb").at(("sq", j)))
                P.emit("vector", lambda En, s0=s0: En.tensor_scalar(
                    stat[:, s0 + 3:s0 + 4], stat[:, s0 + 2:s0 + 3], 1.0 / E,
                    EPS, ALU.mult, ALU.add).then_inc(S("sLNa").h, 1))
                S("sLNa").n += 1
                S("sLNa").mark(("ve", j))
                P.wait("scalar", S("sLNa"), S("sLNa").at(("ve", j)))
                if j >= 4:
                    P.wait("scalar", S("sLNc"), S("sLNc").at(j - 4))
                P.emit("scalar", lambda En, s0=s0: En.activation(
                    stat[:, s0 + 4:s0 + 5], stat[:, s0 + 3:s0 + 4], AF.Ln)
                    .then_inc(S("sLNb").h, 1))
                S("sLNb").n += 1
                P.wait("scalar", S("sLNb"), S("sLNb").n)
                P.emit("scalar", lambda En, s0=s0: En.activation(
                    stat[:, s0 + 5:s0 + 6], stat[:, s0 + 4:s0 + 5], AF.Exp,
                    scale=-0.5).then_inc(S("sLNb").h, 1))
                S("sLNb").n += 1
                S("sLNb").mark(("rs", j))
                P.wait("vector", S("sLNb"), S("sLNb").at(("rs", j)))
                P.emit("vector", lambda En, i=xtb, s0=s0: En.tensor_scalar(
                    i, i, stat[:, s0 + 5:s0 + 6], None, ALU.mult)
                    .then_inc(S("sLNc").h, 1))
                S("sLNc").n += 1
                S("sLNc").mark(j)
                P.wait("tensor", S("sLNc"), S("sLNc").at(j))
                if j >= 2:
                    P.wait("tensor", S("sXNT"), S("sXNT").at(j - 2))
                if j == 0:
                    P.wait("tensor", S("sMISC"), 32)
                for c in range(8):
                    TRN(psP[j % 2][:, c * 128:(c + 1) * 128],
                        xtb[:, c * 128:(c + 1) * 128],
                        128, sem=S("sTR") if c == 7 else None)
                S("sTR").mark(j)
                P.wait("vector", S("sTR"), S("sTR").at(j))
                P.emit("vector", lambda En, j=j: En.tensor_copy(
                    xnt[:].rearrange("p (i t) -> p i t", t=T)
                        [:, :, j * 128:(j + 1) * 128],
                    psP[j % 2][:].rearrange("p (i t) -> p i t", t=128))
                    .then_inc(S("sXNT").h, 1))
                S("sXNT").n += 1
                S("sXNT").mark(j)

            for j in range(NT + 1):
                if j < NT:
                    ln_stage1(j)
                if j >= 1:
                    ln_stage2(j - 1)

            if stage == "ln":
                P.wait("sync", S("sXNT"), S("sXNT").at(NT - 1))
                P.emit("sync", lambda En: En.dma_start(out=dbg_d, in_=xnt[:])
                       .then_inc(S("sMISC").h, 16))
                S("sMISC").n += 16
                P.wait("sync", S("sMISC"), S("sMISC").n)
                P.flush(block)
                return nc

            pieces = []
            for e in range(NEXP):
                g, u, d = psrc(e)
                for r in range(4):
                    pieces.append((g[256 * r:256 * (r + 1), :],
                                   gbf_d[e][:, r * 1024:(r + 1) * 1024]
                                   .rearrange("p (i c) -> p i c", c=512)))
                for r in range(4):
                    pieces.append((u[256 * r:256 * (r + 1), :],
                                   ubf_d[e][:, r * 1024:(r + 1) * 1024]
                                   .rearrange("p (i c) -> p i c", c=512)))
                for ch in range(2):
                    for r in range(2):
                        pieces.append((d[256 * r:256 * (r + 1),
                                         512 * ch:512 * (ch + 1)],
                                       dbf_d[e].rearrange("p (k h) -> p k h", h=1024)
                                       [:, 2 * r:2 * r + 2, ch * 512:(ch + 1) * 512]))
            NPC = len(pieces)

            def precast_step(k):
                b = k % 2
                src, dst = pieces[k]
                fbuf = wcf[:, b * 1024:(b + 1) * 1024]
                bbuf = wcb[:, b * 1024:(b + 1) * 1024]
                P.wait("gpsimd", S("sWCc"), S("sWCc").at(k - 2) if k >= 2 else 0)
                P.emit("gpsimd", lambda En, o=fbuf, i=src: En.dma_start(
                    out=o.rearrange("p (i c) -> p i c", c=512),
                    in_=i.rearrange("(i p) c -> p i c", p=128))
                    .then_inc(Sp("sWCf", k).h, 16))
                Sp("sWCf", k).n += 16
                Sp("sWCf", k).mark(k)
                P.wait("vector", Sp("sWCf", k), Sp("sWCf", k).at(k))
                if k >= 2:
                    P.wait("vector", Sp("sWCo", k), Sp("sWCo", k).at(k - 2))
                P.emit("vector", lambda En, o=bbuf, i=fbuf: En.tensor_copy(o, i)
                       .then_inc(S("sWCc").h, 1))
                S("sWCc").n += 1
                S("sWCc").mark(k)
                P.wait("gpsimd", S("sWCc"), S("sWCc").at(k))
                P.emit("gpsimd", lambda En, o=dst, i=bbuf, kk=k: En.dma_start(
                    out=o, in_=i.rearrange("p (i c) -> p i c", c=512))
                    .then_inc(Sp("sWCo", kk).h, 16))
                Sp("sWCo", k).n += 16
                Sp("sWCo", k).mark(k)

            precast_iter = iter(range(NPC))

            NC2 = T // 512

            def att_prev(p, c, j, k):
                g = (p * NC2 + c) * KT + j - k
                if g < 0:
                    return None
                pp, r = divmod(g, NC2 * KT)
                cc, jj = divmod(r, KT)
                return (pp, cc, jj)

            def S_emit(p, c, j):
                q0 = (p % 2) * T + c * 512
                prev = att_prev(p, c, j, 2)
                if prev is not None:
                    P.wait("tensor", S("sEXP"), S("sEXP").at(prev))
                if c == 0 and j == 0:
                    P.wait("tensor", S("sQKVe"), S("sQKVe").at(("allqkv", p)))
                for hi, ktb in ((0, 0), (1, 64)):
                    MM(psS[j % 2][:, hi * 512:(hi + 1) * 512],
                       kt[ktb:ktb + 64, (p % 2) * T + j * 128:
                          (p % 2) * T + (j + 1) * 128],
                       qt[ktb:ktb + 64, q0:q0 + 512],
                       start=True, stop=True,
                       sem=S("sS") if hi == 1 else None)
                S("sS").mark((p, c, j))

            def EXP_emit(p, c, j):
                P.wait("scalar", S("sS"), S("sS").at((p, c, j)))
                prev = att_prev(p, c, j, 4)
                if prev is not None:
                    P.wait("scalar", S("sPVA"), S("sPVA").at(prev))
                P.emit("scalar", lambda En, j4=j % 4, j2=j % 2: En.activation(
                    es[:, j4 * 1024:(j4 + 1) * 1024], psS[j2][:, 0:1024],
                    AF.Exp, scale=0.125).then_inc(S("sEXP").h, 1))
                S("sEXP").n += 1
                S("sEXP").mark((p, c, j))

            def PV_emit(p, c, jj):
                P.wait("tensor", S("sEXP"), S("sEXP").at((p, c, jj)))
                if jj == 0:
                    P.wait("tensor", S("sONE"), 1)
                    P.wait("tensor", S("sVTe"), S("sVTe").at((p, NT - 1)))
                    if c >= 1:
                        P.wait("tensor", S("sNDA"), S("sNDA").at((p, c - 1)))
                    elif p >= 1:
                        P.wait("tensor", S("sNDA"),
                               S("sNDA").at((p - 1, NC2 - 1)))
                vo = (p % 2) * (NT * 129)
                for hi, voff in ((0, 0), (1, 64)):
                    MM(ps2[0:65, hi * 512:(hi + 1) * 512],
                       vt[:, vo + jj * 129 + voff:vo + jj * 129 + voff + 65],
                       es[:, (jj % 4) * 1024 + hi * 512:
                          (jj % 4) * 1024 + (hi + 1) * 512],
                       start=(jj == 0), stop=(jj == KT - 1),
                       sem=S("sPVA") if hi == 1 else None)
                S("sPVA").mark((p, c, jj))

            ps3c = {"A": None, "B": None}

            def chain_wait(half):
                tok = ps3c[half]
                if tok is not None:
                    P.wait("tensor", tok[0], tok[1])

            def qkv_gen(p):
                b = p % 2
                for (wbuf, dst, pi) in ((wpq, qt, 0), (wpk, kt, 1)):
                    for n in range(NQC):
                        nh = n % 2
                        half = "AB"[nh]
                        chain_wait(half)
                        if pi == 0 and n == 0:
                            P.wait("tensor", S("sWC"), S("sWC").at(("pair", p)))
                        if p == 0:
                            P.wait("tensor", S("sXNT"), S("sXNT").at(
                                min(NT - 1, (n + 1) * 4 - 1)))
                        for c in range(NE):
                            MM(ps3[:, nh * 512:(nh + 1) * 512],
                               wbuf[:, b * 1024 + c * 128:
                                    b * 1024 + (c + 1) * 128],
                               xnt[:, c * T + n * 512:c * T + (n + 1) * 512],
                               start=(c == 0), stop=(c == NE - 1),
                               sem=S("sQKVm") if c == NE - 1 else None)
                        S("sQKVm").mark((p, pi, n))
                        P.wait("vector", S("sQKVm"), S("sQKVm").at((p, pi, n)))
                        if p >= 2 and pi == 0 and n == 0:
                            P.wait("vector", S("sS"), S("sS").at(("pair", p - 2)))
                        P.emit("vector", lambda En, o=dst, n=n, nh=nh, b=b:
                               En.tensor_copy(
                            o[:, b * T + n * 512:b * T + (n + 1) * 512],
                            ps3[:, nh * 512:(nh + 1) * 512])
                            .then_inc(S("sQKVe").h, 1))
                        S("sQKVe").n += 1
                        S("sQKVe").mark((p, pi, n))
                        ps3c[half] = (S("sQKVe"), S("sQKVe").n)
                        yield
                for n in range(NQC):
                    nh = n % 2
                    half = "AB"[nh]
                    chain_wait(half)
                    for c in range(NE):
                        MM(ps3[:, nh * 512:(nh + 1) * 512],
                           wpv[:, b * 1024 + c * 128:b * 1024 + (c + 1) * 128],
                           xnt[:, c * T + n * 512:c * T + (n + 1) * 512],
                           start=(c == 0), stop=(c == NE - 1),
                           sem=S("sQKVm") if c == NE - 1 else None)
                    S("sQKVm").mark((p, "v", n))
                    P.wait("vector", S("sQKVm"), S("sQKVm").at((p, "v", n)))
                    vtf = xt[:, nh * 1024:nh * 1024 + 512]
                    if n >= 2:
                        P.wait("vector", S("sVT"), S("sVT").at((p, (n - 2) * 4 + 3)))
                    elif p >= 1:
                        P.wait("vector", S("sVT"), S("sVT").at(("pair", p - 1)))
                    else:
                        P.wait("vector", S("sLNc"), S("sLNc").n)
                    P.emit("vector", lambda En, o=vtf, nh=nh: En.tensor_copy(
                        o, ps3[:, nh * 512:(nh + 1) * 512])
                        .then_inc(S("sQKVe").h, 1))
                    S("sQKVe").n += 1
                    S("sQKVe").mark((p, "v", n))
                    ps3c[half] = (S("sQKVe"), S("sQKVe").n)
                    yield
                    chain_wait(half)
                    P.wait("tensor", S("sQKVe"), S("sQKVe").at((p, "v", n)))
                    for k in range(4):
                        TRN(ps3[:, nh * 512 + k * 128:nh * 512 + (k + 1) * 128],
                            vtf[:, k * 128:(k + 1) * 128],
                            128, sem=S("sVT") if k == 3 else None)
                        S("sVT").mark((p, n * 4 + k))
                    P.wait("vector", S("sVT"), S("sVT").at((p, n * 4 + 3)))
                    if p >= 2 and n == 0:
                        P.wait("vector", S("sPVA"), S("sPVA").at(("pair", p - 2)))
                    for k in range(4):
                        tt = n * 4 + k
                        col = nh * 512 + k * 128
                        vb = b * (NT * 129)
                        P.emit("vector", lambda En, tt=tt, col=col, vb=vb:
                               En.tensor_copy(vt[:, vb + tt * 129:vb + tt * 129 + 64],
                                              ps3[:, col:col + 64]))
                        P.emit("vector", lambda En, tt=tt, col=col, vb=vb:
                               En.tensor_copy(
                            vt[:, vb + tt * 129 + 65:vb + tt * 129 + 129],
                            ps3[:, col + 64:col + 128])
                            .then_inc(S("sVTe").h, 1))
                        S("sVTe").n += 1
                        S("sVTe").mark((p, tt))
                    ps3c[half] = (S("sVTe"), S("sVTe").n)
                    yield
                S("sVT").mark(("pair", p))
                S("sQKVm").mark(("pairdone", p))
                S("sQKVe").mark(("allqkv", p))

            pend = None
            for p in range(8):
                b = p % 2
                if p + 1 < 8:
                    wpair_load_cast(p + 1)
                if p == 0:
                    for _ in qkv_gen(0):
                        pass
                pend = qkv_gen(p + 1) if p + 1 < 8 else None

                if stage == "qkv" and p == 0:
                    P.wait("sync", S("sQKVe"), S("sQKVe").at(("allqkv", 0)))
                    P.wait("sync", S("sVTe"), S("sVTe").at((0, NT - 1)))
                    P.emit("sync", lambda En: En.dma_start(
                        out=dbg_d[:, 0:T], in_=qt[:]).then_inc(S("sMISC").h, 16))
                    S("sMISC").n += 16
                    P.emit("sync", lambda En: En.dma_start(
                        out=dbg_d[:, T:2 * T], in_=kt[:]).then_inc(S("sMISC").h, 16))
                    S("sMISC").n += 16
                    P.emit("sync", lambda En: En.dma_start(
                        out=dbg_d[:, 2 * T:2 * T + NT * 129], in_=vt[:])
                        .then_inc(S("sMISC").h, 16))
                    S("sMISC").n += 16
                    P.wait("sync", S("sMISC"), S("sMISC").n)
                    P.flush(block)
                    return nc

                for c in range(NC2):
                    pre = 2 if c > 0 else 0
                    for j in range(KT):
                        if j >= pre:
                            S_emit(p, c, j)
                        EXP_emit(p, c, j)
                        if j >= 2:
                            PV_emit(p, c, j - 2)
                        if j % 8 == 7 and pend is not None:
                            next(pend, None)
                    PV_emit(p, c, KT - 2)
                    PV_emit(p, c, KT - 1)
                    if c + 1 < NC2:
                        S_emit(p, c + 1, 0)
                        S_emit(p, c + 1, 1)
                    gidx = p * NC2 + c
                    cs = gidx % 4
                    stg = astg[:, cs * 512:(cs + 1) * 512]
                    P.wait("vector", S("sPVA"), S("sPVA").at((p, c, KT - 1)))
                    P.emit("vector", lambda En: En.tensor_copy(
                        nd[0:65, 0:1024], ps2[0:65, 0:1024])
                        .then_inc(S("sNDA").h, 1))
                    S("sNDA").n += 1
                    S("sNDA").mark((p, c))
                    P.wait("tensor", S("sNDA"), S("sNDA").at((p, c)))
                    chain_wait("A")
                    chain_wait("B")
                    for hi, ndo in ((0, 0), (1, 512)):
                        for t in range(4):
                            TRN(ps3[:, ndo + t * 128:ndo + t * 128 + 65],
                                nd[0:65, ndo + t * 128:ndo + (t + 1) * 128],
                                65, sem=S("sTRA") if (hi == 1 and t == 3)
                                else None)
                    S("sTRA").mark((p, c))
                    P.wait("vector", S("sTRA"), S("sTRA").at((p, c)))
                    P.emit("vector", lambda En: En.tensor_copy(
                        nd[:, 0:1024], ps3[:, 0:1024])
                        .then_inc(S("sDIV").h, 1))
                    S("sDIV").n += 1
                    S("sDIV").mark((p, c))
                    ps3c["A"] = (S("sDIV"), S("sDIV").n)
                    ps3c["B"] = (S("sDIV"), S("sDIV").n)
                    for (hh, ndo) in (("A", 0), ("B", 512)):
                        dcol = 64 if hh == "A" else 0
                        ncol0 = 0 if hh == "A" else 1
                        scol = 0 if hh == "A" else 64
                        if hh == "A":
                            P.wait("vector", S("sDIV"), S("sDIV").at((p, c)))
                        if hh == "A" and gidx >= 4:
                            P.wait("vector", SST(gidx), SST(gidx).at(gidx - 4))
                        for t in range(4):
                            if not (hh == "A" and t == 0):
                                P.wait("vector", S("sDV2"), S("sDV2").n)
                            P.emit("vector", lambda En, ndo=ndo, t=t, dcol=dcol:
                                   En.reciprocal(
                                stat[:, 16:17],
                                nd[:, ndo + t * 128 + dcol:
                                   ndo + t * 128 + dcol + 1])
                                .then_inc(S("sSMv").h, 1))
                            S("sSMv").n += 1
                            P.wait("vector", S("sSMv"), S("sSMv").n)
                            P.emit("vector", lambda En, ndo=ndo, t=t, n0=ncol0,
                                   sc=scol, stg=stg: En.tensor_scalar(
                                stg[:, t * 128 + sc:t * 128 + sc + 64],
                                nd[:, ndo + t * 128 + n0:ndo + t * 128 + n0 + 64],
                                stat[:, 16:17], None, ALU.mult)
                                .then_inc(S("sDV2").h, 1))
                            S("sDV2").n += 1
                    S("sDV2").mark((p, c))
                    P.wait("sync", S("sDV2"), S("sDV2").at((p, c)))
                    P.emit("sync", lambda En, p=p, c=c, stg=stg,
                           sh=SST(gidx).h: En.dma_start(
                        out=attn_d.rearrange("(J p) e -> p J e", p=128)
                            [:, c * 4:(c + 1) * 4, p * 128:(p + 1) * 128],
                        in_=stg.rearrange("p (J e) -> p J e", e=128))
                        .then_inc(sh, 16))
                    SST(gidx).n += 16
                    SST(gidx).mark(gidx)
                    for _ in range(2):
                        k = next(precast_iter, None)
                        if k is not None:
                            precast_step(k)
                if pend is not None:
                    for _ in pend:
                        pass
                S("sEXP").mark(("pair", p))
                S("sS").mark(("pair", p))
                S("sPVA").mark(("pair", p))
            for k in precast_iter:
                precast_step(k)
            for i2 in range(4):
                SST(i2).mark("all")
            for i2 in range(2):
                Sp("sWCo", i2).mark("all")
            S("sDIV").mark("attn_done")
            S("sEXP").mark("attn_done")

            if stage == "attn":
                for i2 in range(4):
                    P.wait("sync", SST(i2), SST(i2).at("all"))
                for i2 in range(2):
                    P.wait("sync", Sp("sWCo", i2), Sp("sWCo", i2).at("all"))
                P.flush(block)
                return nc

            P.flush(block)

        with contextlib.ExitStack() as ph_f:
            psD = ph_f.enter_context(nc.psum_tensor("psD", [128, 2048], F32))
            acc = ph_f.enter_context(nc.sbuf_tensor("acc", [128, 8 * 1024], F32))
            wg = ph_f.enter_context(nc.sbuf_tensor("wg", [128, 2 * 4096], BF16))
            wu = ph_f.enter_context(nc.sbuf_tensor("wu", [128, 2 * 4096], BF16))
            wd = ph_f.enter_context(nc.sbuf_tensor("wd", [128, 2 * 4096], BF16))
            m_sb = ph_f.enter_context(nc.sbuf_tensor("m_sb", [128, 2 * 4096], BF16))
            xres = ph_f.enter_context(nc.sbuf_tensor("xres", [128, 2 * 1024], F32))
            ares = ph_f.enter_context(nc.sbuf_tensor("ares", [128, 2 * 1024], F32))
            sl = ph_f.enter_context(nc.sbuf_tensor("sl", [128, 1024], F32))

            def phase_gate(eng):
                for i2 in range(2):
                    P.wait(eng, Sp("sWCo", i2), Sp("sWCo", i2).at("all"))
                for i2 in range(4):
                    P.wait(eng, SST(i2), SST(i2).at("all"))
                P.wait(eng, S("sDIV"), S("sDIV").at("attn_done"))
                P.wait(eng, S("sEXP"), S("sEXP").at("attn_done"))
                P.wait(eng, S("sWC"), S("sWC").n)
                P.wait(eng, S("sVTe"), S("sVTe").n)
                P.wait(eng, S("sQKVe"), S("sQKVe").n)
            for eng in ("tensor", "vector", "scalar", "sync", "gpsimd"):
                phase_gate(eng)
            for g in range(NT // 4):
                cb = (g % 2) * 1024
                if g >= 2:
                    P.wait("tensor", S("sRTe"), S("sRTe").at(g - 2))
                for t4 in range(4):
                    tt = g * 4 + t4
                    for i in range(NE):
                        MM(psD[:, cb + t4 * 128:cb + t4 * 128 + 8],
                           xnt[:, i * T + tt * 128:i * T + (tt + 1) * 128],
                           rt_bf[:, i * 8:(i + 1) * 8],
                           start=(i == 0), stop=(i == NE - 1),
                           sem=S("sRT") if (t4 == 3 and i == NE - 1) else None)
                S("sRT").mark(g)
                P.wait("vector", S("sRT"), S("sRT").at(g))
                P.emit("vector", lambda En, cb=cb, g=g: En.tensor_copy(
                    logit[:, g * 32:(g + 1) * 32].rearrange("p (t r) -> p t r", r=8),
                    psD[:].rearrange("p (t r) -> p t r", r=128)
                        [:, 8 * (cb // 1024):8 * (cb // 1024) + 4, 0:8])
                    .then_inc(S("sRTe").h, 1))
                S("sRTe").n += 1
                S("sRTe").mark(g)
            S("sRTe").mark("all")
            for tt in range(NT):
                L = logit[:, tt * 8:(tt + 1) * 8]
                P.wait("vector", S("sRTe"), S("sRTe").at(tt // 4))
                c0, c1, c2, c3, c4 = (stat[:, 9:10], stat[:, 10:11], stat[:, 11:12],
                                      stat[:, 12:13], stat[:, 13:14])
                pbuf, ge1, msk, ge2 = (st8[:, 0:8], st8[:, 8:16],
                                       st8[:, 16:24], st8[:, 24:32])
                P.emit("vector", lambda En, L=L, o=c0: En.reduce_max(
                    o, L, axis=AX.X, negate=True).then_inc(S("sSMv").h, 1))
                S("sSMv").n += 1
                P.emit("vector", lambda En, L=L, o=c1: En.reduce_max(
                    o, L, axis=AX.X).then_inc(S("sSMv").h, 1))
                S("sSMv").n += 1
                P.wait("scalar", S("sSMv"), S("sSMv").n)
                P.emit("scalar", lambda En, L=L, o=pbuf, bb=c0, a=c2: En.activation(
                    o, L, AF.Exp, bias=bb, accum_out=a).then_inc(S("sSMs").h, 1))
                S("sSMs").n += 1
                S("sSMs").mark(("exp", tt))
                P.wait("vector", S("sSMs"), S("sSMs").at(("exp", tt)))
                P.emit("vector", lambda En, L=L, o=ge1, m=c1: En.tensor_single_scalar(
                    o, L, m, ALU.is_ge).then_inc(S("sSMv").h, 1))
                S("sSMv").n += 1
                P.wait("vector", S("sSMv"), S("sSMv").n)
                P.emit("vector", lambda En, L=L, o=msk, g1=ge1: En.scalar_tensor_tensor(
                    o, g1, -1e30, L, ALU.mult, ALU.add).then_inc(S("sSMv").h, 1))
                S("sSMv").n += 1
                P.wait("vector", S("sSMv"), S("sSMv").n)
                P.emit("vector", lambda En, o=c3, m=msk: En.reduce_max(
                    o, m, axis=AX.X).then_inc(S("sSMv").h, 1))
                S("sSMv").n += 1
                P.wait("vector", S("sSMv"), S("sSMv").n)
                P.emit("vector", lambda En, L=L, o=ge2, m=c3: En.tensor_single_scalar(
                    o, L, m, ALU.is_ge).then_inc(S("sSMv").h, 1))
                S("sSMv").n += 1
                P.emit("vector", lambda En, o=c4, i=c2: En.reciprocal(o, i)
                       .then_inc(S("sSMv").h, 1))
                S("sSMv").n += 1
                P.wait("vector", S("sSMv"), S("sSMv").n)
                P.emit("vector", lambda En, o=pbuf, r=c4: En.tensor_scalar(
                    o, o, r, None, ALU.mult).then_inc(S("sSMv").h, 1))
                S("sSMv").n += 1
                P.wait("vector", S("sSMv"), S("sSMv").n)
                P.emit("vector", lambda En, o=coef[:, tt * 8:(tt + 1) * 8],
                       a=pbuf, g2=ge2: En.tensor_mul(o, a, g2)
                       .then_inc(S("sSMv").h, 1))
                S("sSMv").n += 1
            S("sSMv").mark("coef_all")

            def ffn_wload(idx, e):
                eb = idx % 2
                for eng in ("scalar",):
                    P.wait(eng, Sp("sWCo", 0), Sp("sWCo", 0).at(12 * e + 10))
                    P.wait(eng, Sp("sWCo", 1), Sp("sWCo", 1).at(12 * e + 11))
                    if idx >= 2:
                        P.wait(eng, S("sTEN"), S("sTEN").at(("edone", idx - 2)))
                for (eng, wdst, wsrc) in (("scalar", wg, gbf_d),
                                          ("scalar", wu, ubf_d),
                                          ("scalar", wd, dbf_d)):
                    P.emit(eng, lambda En, e=e, eb=eb, o=wdst, i=wsrc,
                           sh=Sp("sFW", idx).h: En.dma_start(
                        out=o[:, eb * 4096:(eb + 1) * 4096], in_=i[e])
                        .then_inc(sh, 16))
                    Sp("sFW", idx).n += 16
                Sp("sFW", idx).mark(idx)

            ffn_wload(0, 0)
            ffn_wload(1, 1)
            for cch in range(NCH):
                for t in range(8):
                    tb = t % 2
                    gt = cch * 8 + t
                    if t >= 2:
                        P.wait("sync", S("sINIT"), S("sINIT").at((cch, t - 2)))
                    elif cch >= 1:
                        P.wait("sync", S("sINIT"), S("sINIT").at((cch - 1, 6 + t)))
                    for i2 in range(4):
                        P.wait("sync", SST(i2), SST(i2).at("all"))
                    P.emit("sync", lambda En, gt=gt, tb=tb, sh=Sp("sXR", t).h:
                           En.dma_start(
                        out=xres[:, tb * 1024:(tb + 1) * 1024],
                        in_=x_d[gt * 128:(gt + 1) * 128, :]).then_inc(sh, 16))
                    Sp("sXR", t).n += 16
                    P.emit("sync", lambda En, gt=gt, tb=tb, sh=Sp("sXR", t).h:
                           En.dma_start(
                        out=ares[:, tb * 1024:(tb + 1) * 1024],
                        in_=attn_d[gt * 128:(gt + 1) * 128, :]).then_inc(sh, 16))
                    Sp("sXR", t).n += 16
                    Sp("sXR", t).mark((cch, t))
                    P.wait("vector", Sp("sXR", t), Sp("sXR", t).at((cch, t)))
                    if cch >= 1:
                        P.wait("vector", S("sOUT"), S("sOUT").at(("chunk", cch - 1)))
                    P.emit("vector", lambda En, t=t, tb=tb: En.tensor_add(
                        acc[:, t * 1024:(t + 1) * 1024],
                        xres[:, tb * 1024:(tb + 1) * 1024],
                        ares[:, tb * 1024:(tb + 1) * 1024]).then_inc(S("sINIT").h, 1))
                    S("sINIT").n += 1
                    S("sINIT").mark((cch, t))

                for e in range(NEXP):
                    idx = cch * NEXP + e
                    eb = idx % 2
                    me = idx % 2
                    for k in range(4):
                        for (wsrc, psb, nm) in ((wg, ps2, "g"), (wu, ps3, "u")):
                            if nm == "g":
                                if k >= 1:
                                    P.wait("tensor", S("sSL"), S("sSL").at((idx, k - 1)))
                                elif idx >= 1:
                                    P.wait("tensor", S("sSL"), S("sSL").at((idx - 1, 3)))
                                elif stage == "full":
                                    pass
                            else:
                                if k >= 1:
                                    P.wait("tensor", S("sMUL"),
                                           S("sMUL").at((idx, k - 1)))
                                elif idx >= 1:
                                    P.wait("tensor", S("sMUL"),
                                           S("sMUL").at((idx - 1, 3)))
                            if k == 0 and nm == "g":
                                P.wait("tensor", Sp("sFW", idx), Sp("sFW", idx).at(idx))
                            for i in range(NE):
                                for hf in range(2):
                                    MM(psb[:, hf * 512:(hf + 1) * 512],
                                       wsrc[:, eb * 4096 + i * 512 + k * 128:
                                            eb * 4096 + i * 512 + (k + 1) * 128],
                                       xnt[:, i * T + cch * 1024 + hf * 512:
                                           i * T + cch * 1024 + (hf + 1) * 512],
                                       start=(i == 0), stop=(i == NE - 1),
                                       sem=S("sTEN") if (i == NE - 1 and hf == 1)
                                       else None)
                            S("sTEN").mark((idx, nm, k))
                        P.wait("scalar", S("sTEN"), S("sTEN").at((idx, "g", k)))
                        if k >= 1:
                            P.wait("scalar", S("sMUL"), S("sMUL").at((idx, k - 1)))
                        elif idx >= 1:
                            P.wait("scalar", S("sMUL"), S("sMUL").at((idx - 1, 3)))
                        P.emit("scalar", lambda En: En.activation(
                            sl[:], ps2[:, 0:1024], AF.Silu).then_inc(S("sSL").h, 1))
                        S("sSL").n += 1
                        S("sSL").mark((idx, k))
                        P.wait("vector", S("sSL"), S("sSL").at((idx, k)))
                        P.wait("vector", S("sTEN"), S("sTEN").at((idx, "u", k)))
                        if idx >= 2 and k == 0:
                            P.wait("vector", S("sTEN"), S("sTEN").at(("edone", idx - 2)))
                        P.emit("vector", lambda En, me=me, k=k: En.tensor_mul(
                            m_sb[:, me * 4096 + k * 1024:me * 4096 + (k + 1) * 1024],
                            sl[:], ps3[:, 0:1024]).then_inc(S("sMUL").h, 1))
                        S("sMUL").n += 1
                        S("sMUL").mark((idx, k))
                    for t in range(8):
                        pcol = (t % 2) * 1024
                        if t >= 2:
                            P.wait("tensor", S("sACC"), S("sACC").at((idx, t - 2)))
                        elif idx >= 1:
                            P.wait("tensor", S("sACC"), S("sACC").at((idx - 1, 6 + t)))
                        else:
                            P.wait("tensor", S("sRTe"), S("sRTe").at("all"))
                        if t == 0:
                            P.wait("tensor", S("sMUL"), S("sMUL").at((idx, 3)))
                        for k in range(4):
                            for hf in range(2):
                                MM(psD[:, pcol + hf * 512:pcol + (hf + 1) * 512],
                                   m_sb[:, me * 4096 + k * 1024 + t * 128:
                                        me * 4096 + k * 1024 + (t + 1) * 128],
                                   wd[:, eb * 4096 + k * 1024 + hf * 512:
                                      eb * 4096 + k * 1024 + (hf + 1) * 512],
                                   start=(k == 0), stop=(k == 3),
                                   sem=S("sTEN") if (hf == 1 and k == 3) else None)
                        S("sTEN").mark((idx, "d", t))
                        P.wait("vector", S("sTEN"), S("sTEN").at((idx, "d", t)))
                        P.wait("vector", S("sINIT"), S("sINIT").at((cch, t)))
                        if e >= 2:
                            P.wait("vector", S("sSMv"), S("sSMv").at("coef_all"))
                        if e >= 1:
                            P.wait("vector", S("sACC"), S("sACC").at((idx - 1, t)))
                        gt = cch * 8 + t
                        if e < 2:
                            P.emit("vector", lambda En, pcol=pcol, t=t: En.tensor_add(
                                acc[:, t * 1024:(t + 1) * 1024],
                                acc[:, t * 1024:(t + 1) * 1024],
                                psD[:, pcol:pcol + 1024]).then_inc(S("sACC").h, 1))
                        else:
                            P.emit("vector", lambda En, pcol=pcol, t=t, gt=gt, e=e:
                                   En.scalar_tensor_tensor(
                                acc[:, t * 1024:(t + 1) * 1024],
                                psD[:, pcol:pcol + 1024],
                                coef[:, gt * 8 + (e - 2):gt * 8 + (e - 2) + 1],
                                acc[:, t * 1024:(t + 1) * 1024],
                                ALU.mult, ALU.add).then_inc(S("sACC").h, 1))
                        S("sACC").n += 1
                        S("sACC").mark((idx, t))
                    S("sTEN").mark(("edone", idx))
                    if idx + 2 < NCH * NEXP:
                        ffn_wload(idx + 2, (idx + 2) % NEXP)
                for t in range(8):
                    gt = cch * 8 + t
                    P.wait("sync", S("sACC"), S("sACC").at((cch * NEXP + NEXP - 1, t)))
                    P.emit("sync", lambda En, gt=gt, t=t: En.dma_start(
                        out=out_d[gt * 128:(gt + 1) * 128, :],
                        in_=acc[:, t * 1024:(t + 1) * 1024]).then_inc(S("sOUT").h, 16))
                    S("sOUT").n += 16
                    S("sOUT").mark((cch, t))
                S("sOUT").mark(("chunk", cch))
            P.wait("sync", S("sOUT"), S("sOUT").n)
            P.flush(block)

    return nc


_NC_CACHE = {}


def _get_nc(T):
    if T not in _NC_CACHE:
        _NC_CACHE[T] = build_nc(T)
    return _NC_CACHE[T]


def kernel(L0_attn_x, L0_attn_w_Q, L0_attn_w_K, L0_attn_w_V, L0_ffn_router,
           L0_ffn_experts_gate, L0_ffn_experts_up, L0_ffn_experts_down,
           L0_ffn_shared_gate, L0_ffn_shared_up, L0_ffn_shared_down):
    f32 = lambda a: np.ascontiguousarray(np.asarray(a, dtype=np.float32))
    x = f32(L0_attn_x)
    common = {
        "wq": f32(L0_attn_w_Q), "wk": f32(L0_attn_w_K), "wv": f32(L0_attn_w_V),
        "rt": f32(L0_ffn_router),
        "eg": f32(L0_ffn_experts_gate), "eu": f32(L0_ffn_experts_up),
        "ed": f32(L0_ffn_experts_down),
        "sg": f32(L0_ffn_shared_gate), "su": f32(L0_ffn_shared_up),
        "sd": f32(L0_ffn_shared_down),
        "idf": np.eye(128, dtype=np.float32),
    }
    nc = _get_nc(4096)
    in_maps = [dict(common, x=x[b]) for b in range(8)]
    res = run_bass_kernel_spmd(nc, in_maps, core_ids=list(range(8)))
    return np.stack([res.results[b]["out"] for b in range(8)], axis=0)
